# version 01
# baseline (speedup 1.0000x reference)
import numpy as np
import ml_dtypes
import concourse.bass as bass
import concourse.mybir as mybir
from concourse.bass import IndirectOffsetOnAxis
from concourse.bass_utils import run_bass_kernel_spmd
from contextlib import ExitStack

F32 = mybir.dt.float32
BF16 = mybir.dt.bfloat16
I32 = mybir.dt.int32
AF = mybir.ActivationFunctionType
ALU = mybir.AluOpType
AX = mybir.AxisListType.X

B, T, D = 4, 2048, 2048
HD = 128
II = 1408
E = 8
NTOK = 1024
DC = D // 128
IC = II // 128
QT = T // 128
SCALE = float(1.0 / np.float64(np.sqrt(np.float32(HD))))
NEG = -3.0e38
CAP = 2560.0
EPS = 1e-5
DEBUG = False

SEM_NAMES = ("LD", "LX", "MM", "VE", "SE", "CC", "OUT")


def build_program(plan, final):
    nc = bass.Bass("TRN2", target_bir_lowering=False)
    es = ExitStack()
    C = es.enter_context
    SEM = {n: C(nc.semaphore("s" + n)) for n in SEM_NAMES}
    rec = {}
    cnt = {n: 0 for n in SEM_NAMES}

    def inc(s, key=None, by=1):
        cnt[s] += by
        if key is not None:
            assert key not in rec, key
            rec[key] = cnt[s]
        return cnt[s]

    def mark(key, s):
        rec[key] = cnt[s]

    def W(eng, s, key):
        if final:
            assert key in plan, f"missing milestone {key}"
        v = plan.get(key, 0)
        if v > 0:
            eng.wait_ge(SEM[s], v)

    def DMA(eng, s, out, in_, key=None):
        eng.dma_start(out=out, in_=in_).then_inc(SEM[s], 16)
        inc(s, key, 16)

    def TI(ins, s, key=None):
        ins.then_inc(SEM[s], 1)
        return inc(s, key, 1)

    def FENCE(eng, ins):
        ins.then_inc(SEM["VE"], 1)
        c = inc("VE", None, 1)
        eng.wait_ge(SEM["VE"], c)

    xT = nc.dram_tensor("xT", [D, T], F32, kind="ExternalInput").ap()
    xTo = nc.dram_tensor("xTo", [D, NTOK], F32, kind="ExternalInput").ap()
    wq = nc.dram_tensor("wq", [8, 128, DC * 128], F32, kind="ExternalInput").ap()
    wk = nc.dram_tensor("wk", [8, 128, DC * 128], F32, kind="ExternalInput").ap()
    wv4 = nc.dram_tensor("wv4", [2, 128, DC * 512], F32, kind="ExternalInput").ap()
    wo = nc.dram_tensor("wo", [DC, 128, DC * 128], F32, kind="ExternalInput").ap()
    wr = nc.dram_tensor("wr", [128, DC * 8], F32, kind="ExternalInput").ap()
    wg = nc.dram_tensor("wg", [E, IC, 128, DC * 128], F32, kind="ExternalInput").ap()
    wu = nc.dram_tensor("wu", [E, IC, 128, DC * 128], F32, kind="ExternalInput").ap()
    wd = nc.dram_tensor("wd", [E, DC, 128, IC * 128], F32, kind="ExternalInput").ap()
    idbf = nc.dram_tensor("idbf", [128, 128], BF16, kind="ExternalInput").ap()
    idf32 = nc.dram_tensor("idf32", [128, 128], F32, kind="ExternalInput").ap()
    maskd = nc.dram_tensor("maskd", [128, 128], F32, kind="ExternalInput").ap()
    onesr = nc.dram_tensor("onesr", [1, 128], BF16, kind="ExternalInput").ap()
    onescb = nc.dram_tensor("onescb", [128, 1], BF16, kind="ExternalInput").ap()
    seld = nc.dram_tensor("seld", [8, E * 128], BF16, kind="ExternalInput").ap()
    prefd = nc.dram_tensor("prefd", [64, 8], F32, kind="ExternalInput").ap()
    alld = nc.dram_tensor("alld", [64, 8], F32, kind="ExternalInput").ap()
    zidxd = nc.dram_tensor("zidxd", [128, DC], I32, kind="ExternalInput").ap()
    constf = nc.dram_tensor("constf", [1, 4], F32, kind="ExternalInput").ap()
    y_out = nc.dram_tensor("y_out", [NTOK, D], F32, kind="ExternalOutput").ap()
    zc_in = nc.dram_tensor("zc_in", [2048, NTOK], BF16)
    zc_out = nc.dram_tensor("zc_out", [4096, NTOK], BF16)
    cnt_in = nc.dram_tensor("cnt_in", [8, 2], F32)
    cnt_out = nc.dram_tensor("cnt_out", [64, 2], F32)
    if DEBUG:
        dbg_res1 = nc.dram_tensor("dbg_res1", [128, DC * NTOK], BF16, kind="ExternalOutput").ap()
        dbg_h1 = nc.dram_tensor("dbg_h1", [128, DC * NTOK], BF16, kind="ExternalOutput").ap()
        dbg_comb = nc.dram_tensor("dbg_comb", [1, E * NTOK], BF16, kind="ExternalOutput").ap()
        dbg_y = nc.dram_tensor("dbg_y", [128, DC * NTOK], F32, kind="ExternalOutput").ap()
        dbg_gates = nc.dram_tensor("dbg_gates", [128, 64], F32, kind="ExternalOutput").ap()
        dbg_z = nc.dram_tensor("dbg_z", [128, 8 * T], BF16, kind="ExternalOutput").ap()
        dbg_zcat = nc.dram_tensor("dbg_zcat", [128, DC * NTOK], BF16, kind="ExternalOutput").ap()
        dbg_rtA = nc.dram_tensor("dbg_rtA", [8, 4 * NTOK], F32, kind="ExternalOutput").ap()
        dbg_rtB = nc.dram_tensor("dbg_rtB", [8, 4 * NTOK], F32, kind="ExternalOutput").ap()
        dbg_off = nc.dram_tensor("dbg_off", [8, 4], F32, kind="ExternalOutput").ap()
        dbg_cnta = nc.dram_tensor("dbg_cnta", [64, 2], F32, kind="ExternalOutput").ap()

    id_bf = C(nc.sbuf_tensor("id_bf", [128, 128], BF16))
    id_f = C(nc.sbuf_tensor("id_f", [128, 128], F32))
    mask_sb = C(nc.sbuf_tensor("mask_sb", [128, 128], F32))
    onesr_sb = C(nc.sbuf_tensor("onesr_sb", [1, 128], BF16))
    onesc_sb = C(nc.sbuf_tensor("onesc_sb", [128, 1], BF16))
    sel_sb = C(nc.sbuf_tensor("sel_sb", [8, E * 128], BF16))
    pref_sb = C(nc.sbuf_tensor("pref_sb", [64, 8], F32))
    all_sb = C(nc.sbuf_tensor("all_sb", [64, 8], F32))
    wr_sb = C(nc.sbuf_tensor("wr_sb", [128, DC * 8], BF16))
    zidx_sb = C(nc.sbuf_tensor("zidx_sb", [128, DC], I32))
    constf_sb = C(nc.sbuf_tensor("constf_sb", [1, 4], F32))

    es_att = ExitStack()
    Ca = es_att.enter_context
    K_sb = Ca(nc.sbuf_tensor("K_sb", [128, 8 * T], BF16))
    Q_sb = Ca(nc.sbuf_tensor("Q_sb", [128, 8 * T], BF16))
    V_sb = Ca(nc.sbuf_tensor("V_sb", [128, 2 * 16 * 512], BF16))
    es_qkv = ExitStack()
    Cq = es_qkv.enter_context
    xT_bf = Cq(nc.sbuf_tensor("xT_bf", [128, DC * T], BF16))
    wqk_sb = Cq(nc.sbuf_tensor("wqk_sb", [128, 2 * 2048], BF16))
    wv4_sb = Cq(nc.sbuf_tensor("wv4_sb", [128, 2 * DC * 512], BF16))
    es_psB = ExitStack()
    qk_ps = es_psB.enter_context(nc.psum_tensor("qk_ps", [128, 1024], F32))
    v_ps = es_psB.enter_context(nc.psum_tensor("v_ps", [128, 1024], F32))

    with nc.Block() as blk:
        @blk.sync
        def _(sy):
            for src, dst in ((idbf, id_bf), (idf32, id_f), (maskd, mask_sb),
                             (onesr, onesr_sb), (onescb, onesc_sb), (seld, sel_sb),
                             (prefd, pref_sb), (alld, all_sb), (zidxd, zidx_sb), (constf, constf_sb)):
                DMA(sy, "LX", dst[:, :], src)
            mark("lx_consts", "LX")

        @blk.gpsimd
        def _(g):
            DMA(g, "LD", wr_sb[:, :], wr, "ld_wr")

    with nc.Block() as blk:
        @blk.gpsimd
        def _(g):
            for c in range(DC):
                DMA(g, "LD", xT_bf[:, c * T:(c + 1) * T], xT[c * 128:(c + 1) * 128, :])
            mark("ld_x", "LD")
            for h4 in range(2):
                DMA(g, "LD", wv4_sb[:, h4 * DC * 512:(h4 + 1) * DC * 512], wv4[h4])
            mark("ld_wv4", "LD")
            for k in range(16):
                if k >= 2:
                    W(g, "MM", f"mm_qkw_{k - 2}")
                DMA(g, "LD", wqk_sb[:, (k % 2) * 2048:(k % 2 + 1) * 2048],
                    (wq if k % 2 == 0 else wk)[k // 2], f"ld_qkw_{k}")

        @blk.tensor
        def _(t):
            W(t, "LD", "ld_x")
            for k in range(16):
                W(t, "LD", f"ld_qkw_{k}")
                for m in range(4):
                    pass
                for npair in range(2):
                    gidx = 2 * k + npair
                    if gidx >= 1:
                        W(t, "SE", f"se_qk_{gidx - 1}")
                    for c in range(DC):
                        for ni in range(2):
                            n = 2 * npair + ni
                            ins = t.matmul(qk_ps[:, ni * 512:(ni + 1) * 512],
                                           wqk_sb[:, (k % 2) * 2048 + c * 128:(k % 2) * 2048 + (c + 1) * 128],
                                           xT_bf[:, c * T + n * 512:c * T + (n + 1) * 512],
                                           start=(c == 0), stop=(c == DC - 1))
                    TI(ins, "MM", f"mm_qk_{gidx}")
                mark(f"mm_qkw_{k}", "MM")
            W(t, "LD", "ld_wv4")
            for j in range(32):
                h4, st = j // 16, j % 16
                if j >= 2:
                    W(t, "VE", f"ve_v_{j - 2}")
                for c in range(DC):
                    ins = t.matmul(v_ps[:, (j % 2) * 512:(j % 2 + 1) * 512],
                                   xT_bf[:, c * T + st * 128:c * T + (st + 1) * 128],
                                   wv4_sb[:, h4 * DC * 512 + c * 512:h4 * DC * 512 + (c + 1) * 512],
                                   start=(c == 0), stop=(c == DC - 1))
                TI(ins, "MM", f"mm_v_{j}")
            mark("mm_B", "MM")

        @blk.scalar
        def _(s):
            for k in range(16):
                dest = Q_sb if k % 2 == 0 else K_sb
                h = k // 2
                for npair in range(2):
                    gidx = 2 * k + npair
                    W(s, "MM", f"mm_qk_{gidx}")
                    if k % 2 == 0:
                        ins = s.activation(dest[:, h * T + npair * 1024:h * T + (npair + 1) * 1024],
                                           qk_ps[:, :], AF.Copy, bias=0.0, scale=SCALE)
                    else:
                        ins = s.copy(dest[:, h * T + npair * 1024:h * T + (npair + 1) * 1024],
                                     qk_ps[:, :])
                    TI(ins, "SE", f"se_qk_{gidx}")
            mark("se_B", "SE")

        @blk.vector
        def _(v):
            for j in range(32):
                W(v, "MM", f"mm_v_{j}")
                ins = v.tensor_copy(V_sb[:, j * 512:(j + 1) * 512],
                                    v_ps[:, (j % 2) * 512:(j % 2 + 1) * 512])
                TI(ins, "VE", f"ve_v_{j}")
            mark("ve_B", "VE")

    es_qkv.close()
    es_psB.close()

    es_z = ExitStack()
    Z_sb = es_z.enter_context(nc.sbuf_tensor("Z_sb", [128, 8 * T], BF16, side="right"))
    es_attc = ExitStack()
    Cc = es_attc.enter_context
    pf_sb = Cc(nc.sbuf_tensor("pf_sb", [128, 2 * 2048], F32))
    pb_sb = Cc(nc.sbuf_tensor("pb_sb", [128, 2 * 2048], BF16))
    ptr_sb = Cc(nc.sbuf_tensor("ptr_sb", [128, 2 * 1024], BF16))
    nm_sb = Cc(nc.sbuf_tensor("nm_sb", [128, 4], F32))
    scr_sb = Cc(nc.sbuf_tensor("scr_sb", [128, 4], F32))
    es_psC = ExitStack()
    s_ps = es_psC.enter_context(nc.psum_tensor("s_ps", [128, 2048], F32))
    ptr_ps = [es_psC.enter_context(nc.psum_tensor(f"ptr_ps{i}", [128, 1024], BF16)) for i in range(2)]
    z_ps = [es_psC.enter_context(nc.psum_tensor(f"z_ps{i}", [128, 512], F32)) for i in range(2)]

    iters = [(h, qt) for h in range(8) for qt in range(QT)]
    NIT = len(iters)

    def grps(vis):
        out = [(0, min(8, vis))]
        if vis > 8:
            out.append((8, vis - 8))
        return out

    with nc.Block() as blk:
        @blk.tensor
        def _(t):
            W(t, "SE", "se_B")
            W(t, "VE", "ve_B")
            for step in range(NIT + 1):
                if step < NIT:
                    h, qt = iters[step]
                    vis = qt + 1
                    if step >= 1:
                        W(t, "SE", f"se_exp_{step - 1}")
                    nw = (vis * 128 + 511) // 512
                    for n in range(nw):
                        wdt = min(512, vis * 128 - n * 512)
                        ins = t.matmul(s_ps[:, n * 512:n * 512 + wdt],
                                       Q_sb[:, h * T + qt * 128:h * T + (qt + 1) * 128],
                                       K_sb[:, h * T + n * 512:h * T + n * 512 + wdt],
                                       start=True, stop=True)
                    TI(ins, "MM", f"mm_s_{step}")
                if step >= 1:
                    p = step - 1
                    ph, pqt = iters[p]
                    pvis = pqt + 1
                    h4p, hhp = ph // 4, ph % 4
                    W(t, "SE", f"se_pcast_{p}")
                    for gi, (o, gl) in enumerate(grps(pvis)):
                        if p >= 1:
                            W(t, "VE", f"ve_ptrev_{p - 1}_{gi}")
                        for sc in range(o, o + gl):
                            ins = t.transpose(ptr_ps[gi][:, (sc - o) * 128:(sc - o + 1) * 128],
                                              pb_sb[:, (p % 2) * 2048 + sc * 128:(p % 2) * 2048 + (sc + 1) * 128],
                                              id_bf[:, :])
                        TI(ins, "MM", f"mm_ptr_{p}_{gi}")
                    zg = p // 4
                    if pqt % 4 == 0 and zg >= 2:
                        W(t, "VE", f"ve_zev_{zg - 2}")
                    for gi, (o, gl) in enumerate(grps(pvis)):
                        W(t, "VE", f"ve_ptrevc_{p}_{gi}")
                        for sc in range(o, o + gl):
                            ins = t.matmul(z_ps[zg % 2][:, (pqt % 4) * 128:(pqt % 4 + 1) * 128],
                                           V_sb[:, h4p * 16 * 512 + sc * 512 + hhp * 128:
                                                h4p * 16 * 512 + sc * 512 + (hhp + 1) * 128],
                                           ptr_sb[:, gi * 1024 + (sc - o) * 128:gi * 1024 + (sc - o + 1) * 128],
                                           start=(sc == 0), stop=(sc == pvis - 1))
                    TI(ins, "MM", f"mm_pv_{p}")
            mark("mm_C", "MM")

        @blk.vector
        def _(v):
            for step in range(NIT + 1):
                if step < NIT:
                    h, qt = iters[step]
                    vis = qt + 1
                    W(v, "MM", f"mm_s_{step}")
                    FENCE(v, v.tensor_tensor(s_ps[:, (vis - 1) * 128:vis * 128],
                                              s_ps[:, (vis - 1) * 128:vis * 128], mask_sb[:, :], ALU.add))
                    ins = v.reduce_max(nm_sb[:, 0:1], s_ps[:, 0:vis * 128], AX, negate=True)
                    TI(ins, "VE", f"ve_stats_{step}")
                if step >= 1:
                    p = step - 1
                    gg = grps(iters[p][1] + 1)
                    W(v, "MM", f"mm_ptr_{p}_0")
                    if p >= 1:
                        W(v, "MM", f"mm_pv_{p - 1}")
                    o, gl = gg[0]
                    ins = v.tensor_copy(ptr_sb[:, 0:gl * 128], ptr_ps[0][:, 0:gl * 128])
                    TI(ins, "VE", f"ve_ptrevc_{p}_0")
                    mark(f"ve_ptrev_{p}_0", "VE")
                if step < NIT:
                    h, qt = iters[step]
                    vis = qt + 1
                    W(v, "SE", f"se_exp_{step}")
                    FENCE(v, v.reduce_sum(nm_sb[:, 2:3],
                                          pf_sb[:, (step % 2) * 2048:(step % 2) * 2048 + vis * 128], AX))
                    ins = v.reciprocal(nm_sb[:, 3:4], nm_sb[:, 2:3])
                    TI(ins, "VE", f"ve_recip_{step}")
                if step >= 1:
                    p = step - 1
                    ph, pqt = iters[p]
                    gg = grps(pqt + 1)
                    if len(gg) > 1:
                        W(v, "MM", f"mm_ptr_{p}_1")
                        o, gl = gg[1]
                        ins = v.tensor_copy(ptr_sb[:, 1024:1024 + gl * 128], ptr_ps[1][:, 0:gl * 128])
                        TI(ins, "VE", f"ve_ptrevc_{p}_1")
                        mark(f"ve_ptrev_{p}_1", "VE")
                    else:
                        mark(f"ve_ptrevc_{p}_1", "VE")
                        mark(f"ve_ptrev_{p}_1", "VE")
                    if pqt % 4 == 3:
                        W(v, "MM", f"mm_pv_{p}")
                        ins = v.tensor_copy(Z_sb[:, ph * T + (pqt - 3) * 128:ph * T + (pqt + 1) * 128],
                                            z_ps[(p // 4) % 2][:, :])
                        TI(ins, "VE", f"ve_zev_{p // 4}")
            mark("ve_C", "VE")

        @blk.scalar
        def _(s):
            for step in range(NIT):
                h, qt = iters[step]
                vis = qt + 1
                W(s, "VE", f"ve_stats_{step}")
                ins = s.activation(pf_sb[:, (step % 2) * 2048:(step % 2) * 2048 + vis * 128],
                                   s_ps[:, 0:vis * 128], AF.Exp,
                                   bias=nm_sb[:, 0:1], scale=1.0)
                TI(ins, "SE", f"se_exp_{step}")
                W(s, "VE", f"ve_recip_{step}")
                if step >= 2:
                    ngr = len(grps(iters[step - 2][1] + 1))
                    W(s, "MM", f"mm_ptr_{step - 2}_{ngr - 1}")
                ins = s.activation(pb_sb[:, (step % 2) * 2048:(step % 2) * 2048 + vis * 128],
                                   pf_sb[:, (step % 2) * 2048:(step % 2) * 2048 + vis * 128],
                                   AF.Copy, bias=0.0, scale=nm_sb[:, 3:4])
                TI(ins, "SE", f"se_pcast_{step}")
            mark("se_C", "SE")

    es_attc.close()
    es_psC.close()
    es_att.close()

    es_d = ExitStack()
    Cd = es_d.enter_context
    h1_sb = C(nc.sbuf_tensor("h1_sb", [128, DC * NTOK], BF16))
    Zcat = Cd(nc.sbuf_tensor("Zcat", [128, DC * NTOK], BF16))
    xo_sb = Cd(nc.sbuf_tensor("xo_sb", [128, 4 * NTOK], F32))
    res1_sb = Cd(nc.sbuf_tensor("res1_sb", [128, DC * NTOK], BF16))
    wo_sb = Cd(nc.sbuf_tensor("wo_sb", [128, 2 * 2048], BF16))
    sq_sb = Cd(nc.sbuf_tensor("sq_sb", [128, 2 * NTOK], BF16))
    r_mean = Cd(nc.sbuf_tensor("r_mean", [1, NTOK], F32))
    r_var = Cd(nc.sbuf_tensor("r_var", [1, NTOK], F32))
    pack = Cd(nc.sbuf_tensor("pack", [1, 2 * NTOK], F32))
    pack_bf = Cd(nc.sbuf_tensor("pack_bf", [1, 2 * NTOK], BF16))
    tmp_f = Cd(nc.sbuf_tensor("tmp_f", [128, 2 * NTOK], F32))
    es_ps_s = ExitStack()
    sums_ps = es_ps_s.enter_context(nc.psum_tensor("sums_ps", [1, NTOK], F32))
    sqs_ps = es_ps_s.enter_context(nc.psum_tensor("sqs_ps", [1, NTOK], F32))
    es_ps_at = ExitStack()
    at_ps = [es_ps_at.enter_context(nc.psum_tensor(f"at_ps{i}", [128, NTOK], F32)) for i in range(2)]

    with nc.Block() as blk:
        @blk.sync
        def _(sy):
            for half in range(2):
                for h in range(8):
                    W(sy, "VE", f"ve_zev_{h * 4 + 3}")
                    DMA(sy, "LX", zc_in[half * 1024 + h * 128:half * 1024 + (h + 1) * 128, :],
                        Z_sb[:, h * T + half * NTOK:h * T + half * NTOK + NTOK])
            mark("lx_zstage", "LX")
            if DEBUG:
                DMA(sy, "LX", dbg_z, Z_sb[:, :])
            W(sy, "MM", "mm_C")
            W(sy, "SE", "se_C")
            for ct in range(DC):
                if ct >= 4:
                    W(sy, "VE", f"ve_res1_{ct - 4}")
                DMA(sy, "LX", xo_sb[:, (ct % 4) * NTOK:(ct % 4 + 1) * NTOK],
                    xTo[ct * 128:(ct + 1) * 128, :], f"lx_xo_{ct}")
            if DEBUG:
                W(sy, "LD", "ld_zcat")
                DMA(sy, "LX", dbg_zcat, Zcat[:, :])
                W(sy, "VE", "ve_res1_all")
                DMA(sy, "LX", dbg_res1, res1_sb[:, :], "lx_dbgD")

        @blk.gpsimd
        def _(g):
            W(g, "MM", "mm_C")
            W(g, "SE", "se_C")
            W(g, "LX", "lx_zstage")
            g.collective_compute("AllGather", ALU.bypass,
                                 replica_groups=[[0, 1], [2, 3], [4, 5], [6, 7]],
                                 ins=[zc_in.ap().opt()], outs=[zc_out.ap().opt()]).then_inc(SEM["CC"], 1)
            inc("CC", "cc_z")
            for ot in range(2):
                DMA(g, "LD", wo_sb[:, (ot % 2) * 2048:(ot % 2 + 1) * 2048], wo[ot], f"ld_wo_{ot}")
            W(g, "CC", "cc_z")
            for ctk in range(DC):
                g.indirect_dma_start(
                    out=Zcat[:, ctk * NTOK:(ctk + 1) * NTOK],
                    out_offset=None,
                    in_=zc_out.ap(),
                    in_offset=IndirectOffsetOnAxis(ap=zidx_sb[:, ctk:ctk + 1], axis=0),
                ).then_inc(SEM["LD"], 16)
                inc("LD", None, 16)
            mark("ld_zcat", "LD")
            for ot in range(2, DC):
                W(g, "MM", f"mm_at_{ot - 2}")
                DMA(g, "LD", wo_sb[:, (ot % 2) * 2048:(ot % 2 + 1) * 2048], wo[ot], f"ld_wo_{ot}")

        @blk.tensor
        def _(t):
            W(t, "LD", "ld_zcat")
            for ot in range(DC):
                W(t, "LD", f"ld_wo_{ot}")
                if ot >= 2:
                    W(t, "VE", f"ve_res1_{ot - 2}")
                for c in range(DC):
                    for n in range(2):
                        ins = t.matmul(at_ps[ot % 2][:, n * 512:(n + 1) * 512],
                                       wo_sb[:, (ot % 2) * 2048 + c * 128:(ot % 2) * 2048 + (c + 1) * 128],
                                       Zcat[:, c * NTOK + n * 512:c * NTOK + (n + 1) * 512],
                                       start=(c == 0), stop=(c == DC - 1))
                TI(ins, "MM", f"mm_at_{ot}")
            for po in range(DC):
                W(t, "VE", f"ve_res1_{po}")
                for n in range(2):
                    ins = t.matmul(sums_ps[0:1, n * 512:(n + 1) * 512], onesc_sb[:, 0:1],
                                   res1_sb[:, po * NTOK + n * 512:po * NTOK + (n + 1) * 512],
                                   start=(po == 0), stop=(po == DC - 1))
                TI(ins, "MM", f"mm_sum_{po}")
                W(t, "SE", f"se_sq_{po}")
                for n in range(2):
                    ins = t.matmul(sqs_ps[0:1, n * 512:(n + 1) * 512], onesc_sb[:, 0:1],
                                   sq_sb[:, (po % 2) * NTOK + n * 512:(po % 2) * NTOK + (n + 1) * 512],
                                   start=(po == 0), stop=(po == DC - 1))
                TI(ins, "MM", f"mm_sq_{po}")
            mark("mm_lnsums", "MM")

        @blk.vector
        def _(v):
            for ot in range(DC):
                W(v, "MM", f"mm_at_{ot}")
                W(v, "LX", f"lx_xo_{ot}")
                ins = v.tensor_tensor(res1_sb[:, ot * NTOK:(ot + 1) * NTOK],
                                      at_ps[ot % 2][:, :],
                                      xo_sb[:, (ot % 4) * NTOK:(ot % 4 + 1) * NTOK], ALU.add)
                TI(ins, "VE", f"ve_res1_{ot}")
            mark("ve_res1_all", "VE")

        @blk.scalar
        def _(s):
            for ot in range(DC):
                W(s, "VE", f"ve_res1_{ot}")
                if ot >= 2:
                    W(s, "MM", f"mm_sq_{ot - 2}")
                ins = s.square(sq_sb[:, (ot % 2) * NTOK:(ot % 2 + 1) * NTOK],
                               res1_sb[:, ot * NTOK:(ot + 1) * NTOK])
                TI(ins, "SE", f"se_sq_{ot}")

    es_ps_at.close()
    es_ps_ab = ExitStack()
    ab_ps = es_ps_ab.enter_context(nc.psum_tensor("ab_ps", [128, 2048], F32))

    with nc.Block() as blk:
        @blk.vector
        def _(v):
            W(v, "MM", "mm_lnsums")
            FENCE(v, v.tensor_scalar_mul(r_mean[0:1, :], sums_ps[0:1, :], 1.0 / D))
            FENCE(v, v.tensor_scalar_mul(r_var[0:1, :], sqs_ps[0:1, :], 1.0 / D))
            FENCE(v, v.tensor_tensor(pack[0:1, 0:NTOK], r_mean[0:1, :], r_mean[0:1, :], ALU.mult))
            ins = v.tensor_tensor(r_var[0:1, :], r_var[0:1, :], pack[0:1, 0:NTOK], ALU.subtract)
            TI(ins, "VE", "ve_var1")
            W(v, "SE", "se_std1")
            FENCE(v, v.reciprocal(pack[0:1, 0:NTOK], pack[0:1, NTOK:2 * NTOK]))
            FENCE(v, v.scalar_tensor_tensor(pack[0:1, NTOK:2 * NTOK], r_mean[0:1, :], -1.0,
                                            pack[0:1, 0:NTOK], ALU.mult, ALU.mult))
            ins = v.tensor_copy(pack_bf[0:1, :], pack[0:1, :])
            TI(ins, "VE", "ve_lnrows1")
            W(v, "MM", "mm_ab1")
            for ct in range(DC):
                v.tensor_tensor(tmp_f[:, (ct % 2) * NTOK:(ct % 2 + 1) * NTOK],
                                res1_sb[:, ct * NTOK:(ct + 1) * NTOK],
                                ab_ps[:, 0:NTOK], ALU.mult)
                ins = v.tensor_tensor(h1_sb[:, ct * NTOK:(ct + 1) * NTOK],
                                      tmp_f[:, (ct % 2) * NTOK:(ct % 2 + 1) * NTOK],
                                      ab_ps[:, NTOK:2 * NTOK], ALU.add)
                TI(ins, "VE", f"ve_h1_{ct}")
            mark("ve_h1", "VE")

        @blk.scalar
        def _(s):
            W(s, "VE", "ve_var1")
            ins = s.activation(pack[0:1, NTOK:2 * NTOK], r_var[0:1, :], AF.Sqrt,
                               bias=constf_sb[0:1, 0:1], scale=1.0)
            TI(ins, "SE", "se_std1")

        @blk.tensor
        def _(t):
            W(t, "VE", "ve_lnrows1")
            for n in range(4):
                ins = t.matmul(ab_ps[:, n * 512:(n + 1) * 512], onesr_sb[0:1, :],
                               pack_bf[0:1, n * 512:(n + 1) * 512], start=True, stop=True)
            TI(ins, "MM", "mm_ab1")

    es_d.close()
    es_z.close()
    es_ps_ab.close()
    es_ps_s.close()
    combbc_sb = C(nc.sbuf_tensor("combbc_sb", [128, E * NTOK], BF16))
    es_e = ExitStack()
    Ce = es_e.enter_context
    gates_sb = Ce(nc.sbuf_tensor("gates_sb", [128, 64], F32))
    oh1_sb = Ce(nc.sbuf_tensor("oh1_sb", [128, 64], BF16))
    oh2_sb = Ce(nc.sbuf_tensor("oh2_sb", [128, 64], BF16))
    v1_sb = Ce(nc.sbuf_tensor("v1_sb", [128, 64], BF16))
    v2_sb = Ce(nc.sbuf_tensor("v2_sb", [128, 64], BF16))
    sc8_sb = Ce(nc.sbuf_tensor("sc8_sb", [128, 8], F32))
    gm_sb = Ce(nc.sbuf_tensor("gm_sb", [128, 8], F32))
    rtA = Ce(nc.sbuf_tensor("rtA", [8, 4 * NTOK], F32))
    rtB = Ce(nc.sbuf_tensor("rtB", [8, 4 * NTOK], F32))
    combT_bf = Ce(nc.sbuf_tensor("combT_bf", [8, NTOK], BF16))
    cnt_sb = Ce(nc.sbuf_tensor("cnt_sb", [8, 2], F32))
    cnta_sb = Ce(nc.sbuf_tensor("cnta_sb", [64, 2], F32))
    off_sb = Ce(nc.sbuf_tensor("off_sb", [8, 4], F32))
    es_psE = ExitStack()
    lg_ps = es_psE.enter_context(nc.psum_tensor("lg_ps", [128, 64], F32))
    trE_ps = [es_psE.enter_context(nc.psum_tensor(f"trE_ps{i}", [8, 512], F32)) for i in range(2)]
    pmA_ps = es_psE.enter_context(nc.psum_tensor("pmA_ps", [8, 2], F32))
    pmB_ps = es_psE.enter_context(nc.psum_tensor("pmB_ps", [8, 2], F32))
    cb_ps = es_psE.enter_context(nc.psum_tensor("cb_ps", [128, NTOK], F32))

    with nc.Block() as blk:
        @blk.tensor
        def _(t):
            W(t, "VE", "ve_h1")
            for tt in range(8):
                for c in range(DC):
                    ins = t.matmul(lg_ps[:, tt * 8:(tt + 1) * 8],
                                   h1_sb[:, c * NTOK + tt * 128:c * NTOK + (tt + 1) * 128],
                                   wr_sb[:, c * 8:(c + 1) * 8],
                                   start=(c == 0), stop=(c == DC - 1))
            TI(ins, "MM", "mm_lg_all")
            W(t, "VE", "ve_top2all")
            for tt in range(8):
                if tt >= 2:
                    W(t, "VE", f"ve_trev_{tt - 2}")
                for ki, srcT in enumerate((oh1_sb, oh2_sb, v1_sb, v2_sb)):
                    ins = t.matmul(trE_ps[tt % 2][0:8, ki * 128:(ki + 1) * 128],
                                   srcT[:, tt * 8:(tt + 1) * 8], id_bf[:, :],
                                   start=True, stop=True)
                TI(ins, "MM", f"mm_tr_{tt}")
            W(t, "LD", "ld_cnta")
            t.matmul(pmA_ps[0:8, 0:2], pref_sb[:, :], cnta_sb[:, :], start=True, stop=True)
            ins = t.matmul(pmB_ps[0:8, 0:2], all_sb[:, :], cnta_sb[:, :], start=True, stop=True)
            TI(ins, "MM", "mm_pref")
            for e in range(E):
                W(t, "VE", "ve_combT")
                if e >= 1:
                    W(t, "SE", f"se_cb_{e - 1}")
                for n in range(2):
                    ins = t.matmul(cb_ps[:, n * 512:(n + 1) * 512],
                                   sel_sb[:, e * 128:(e + 1) * 128],
                                   combT_bf[0:8, n * 512:(n + 1) * 512],
                                   start=True, stop=True)
                TI(ins, "MM", f"mm_cb_{e}")
            mark("mm_E", "MM")

        @blk.vector
        def _(v):
            W(v, "MM", "mm_lg_all")
            ins = v.reduce_max(sc8_sb[:, 0:1], lg_ps[:, :], AX, negate=True)
            TI(ins, "VE", "ve_lgm")
            W(v, "SE", "se_gexp")
            for tt in range(8):
                g8 = gates_sb[:, tt * 8:(tt + 1) * 8]
                FENCE(v, v.reduce_sum(sc8_sb[:, 1:2], g8, AX))
                FENCE(v, v.reciprocal(sc8_sb[:, 2:3], sc8_sb[:, 1:2]))
                FENCE(v, v.tensor_scalar_mul(g8, g8, sc8_sb[:, 2:3]))
                FENCE(v, v.reduce_max(sc8_sb[:, 3:4], g8, AX))
                FENCE(v, v.tensor_single_scalar(oh1_sb[:, tt * 8:(tt + 1) * 8], g8, sc8_sb[:, 3:4], ALU.is_equal))
                FENCE(v, v.tensor_tensor(gm_sb[:, :], g8, oh1_sb[:, tt * 8:(tt + 1) * 8], ALU.mult))
                FENCE(v, v.tensor_tensor(gm_sb[:, :], g8, gm_sb[:, :], ALU.subtract))
                FENCE(v, v.reduce_max(sc8_sb[:, 4:5], gm_sb[:, :], AX))
                FENCE(v, v.tensor_single_scalar(oh2_sb[:, tt * 8:(tt + 1) * 8], gm_sb[:, :], sc8_sb[:, 4:5], ALU.is_equal))
                FENCE(v, v.tensor_single_scalar(v1_sb[:, tt * 8:(tt + 1) * 8], oh1_sb[:, tt * 8:(tt + 1) * 8],
                                                sc8_sb[:, 3:4], ALU.mult))
                FENCE(v, v.tensor_single_scalar(v2_sb[:, tt * 8:(tt + 1) * 8], oh2_sb[:, tt * 8:(tt + 1) * 8],
                                                sc8_sb[:, 4:5], ALU.mult))
            ins = v.reduce_sum(sc8_sb[:, 5:6], h1_sb[:, 0:NTOK], AX)
            TI(ins, "VE", "ve_top2all")
            for tt in range(8):
                W(v, "MM", f"mm_tr_{tt}")
                for ki in range(4):
                    ins = v.tensor_copy(rtA[0:8, ki * NTOK + tt * 128:ki * NTOK + (tt + 1) * 128],
                                        trE_ps[tt % 2][0:8, ki * 128:(ki + 1) * 128])
                TI(ins, "VE", f"ve_trev_{tt}")
            FENCE(v, v.tensor_tensor_scan(rtB[0:8, 0:NTOK], rtA[0:8, 0:NTOK], rtA[0:8, 0:NTOK],
                                          0.0, ALU.add, ALU.bypass))
            FENCE(v, v.tensor_tensor_scan(rtB[0:8, NTOK:2 * NTOK], rtA[0:8, NTOK:2 * NTOK],
                                          rtA[0:8, NTOK:2 * NTOK], 0.0, ALU.add, ALU.bypass))
            FENCE(v, v.tensor_copy(cnt_sb[0:8, 0:1], rtB[0:8, NTOK - 1:NTOK]))
            FENCE(v, v.tensor_copy(cnt_sb[0:8, 1:2], rtB[0:8, 2 * NTOK - 1:2 * NTOK]))
            ins = v.reduce_sum(sc8_sb[0:8, 6:7], rtB[0:8, 0:NTOK], AX)
            TI(ins, "VE", "ve_cnt")
            W(v, "MM", "mm_pref")
            FENCE(v, v.tensor_copy(off_sb[0:8, 0:2], pmA_ps[0:8, :]))
            FENCE(v, v.tensor_copy(off_sb[0:8, 2:3], pmB_ps[0:8, 0:1]))
            ins = v.tensor_tensor(off_sb[0:8, 3:4], off_sb[0:8, 2:3], off_sb[0:8, 1:2], ALU.add)
            FENCE(v, ins)
            mark("ve_off", "VE")
            FENCE(v, v.tensor_scalar(rtB[0:8, 0:NTOK], rtB[0:8, 0:NTOK], off_sb[0:8, 0:1], CAP,
                                     ALU.add, ALU.is_le))
            FENCE(v, v.tensor_scalar(rtB[0:8, NTOK:2 * NTOK], rtB[0:8, NTOK:2 * NTOK], off_sb[0:8, 3:4], CAP,
                                     ALU.add, ALU.is_le))
            FENCE(v, v.tensor_tensor(rtB[0:8, 2 * NTOK:3 * NTOK], rtA[0:8, 2 * NTOK:3 * NTOK],
                                     rtB[0:8, 0:NTOK], ALU.mult))
            FENCE(v, v.tensor_tensor(rtB[0:8, 3 * NTOK:4 * NTOK], rtA[0:8, 3 * NTOK:4 * NTOK],
                                     rtB[0:8, NTOK:2 * NTOK], ALU.mult))
            FENCE(v, v.tensor_tensor(rtB[0:8, 2 * NTOK:3 * NTOK], rtB[0:8, 2 * NTOK:3 * NTOK],
                                     rtB[0:8, 3 * NTOK:4 * NTOK], ALU.add))
            ins = v.tensor_copy(combT_bf[0:8, :], rtB[0:8, 2 * NTOK:3 * NTOK])
            TI(ins, "VE", "ve_combT")
            mark("ve_E", "VE")

        @blk.scalar
        def _(s):
            W(s, "VE", "ve_lgm")
            s.activation(gates_sb[:, :], lg_ps[:, :],
                         AF.Exp, bias=sc8_sb[:, 0:1], scale=1.0)
            ins = s.copy(gm_sb[:, :], gates_sb[:, 0:8])
            TI(ins, "SE", "se_gexp")
            for e in range(E):
                W(s, "MM", f"mm_cb_{e}")
                ins = s.copy(combbc_sb[:, e * NTOK:(e + 1) * NTOK], cb_ps[:, :])
                TI(ins, "SE", f"se_cb_{e}")
            mark("se_E", "SE")

        if DEBUG:
            @blk.sync
            def _(sy):
                W(sy, "VE", "ve_combT")
                W(sy, "LD", "ld_cnta")
                DMA(sy, "LX", dbg_gates, gates_sb[:, :])
                DMA(sy, "LX", dbg_rtA, rtA[0:8, :])
                DMA(sy, "LX", dbg_rtB, rtB[0:8, :])
                DMA(sy, "LX", dbg_off, off_sb[0:8, :])
                DMA(sy, "LX", dbg_cnta, cnta_sb[:, :], "lx_dbgE")

        @blk.gpsimd
        def _(g):
            W(g, "VE", "ve_cnt")
            DMA(g, "LD", cnt_in.ap(), cnt_sb[0:8, :], "ld_cntin")
            W(g, "LD", "ld_cntin")
            g.collective_compute("AllGather", ALU.bypass,
                                 replica_groups=[[0, 1, 2, 3, 4, 5, 6, 7]],
                                 ins=[cnt_in.ap().opt()], outs=[cnt_out.ap().opt()]).then_inc(SEM["CC"], 1)
            inc("CC", "cc_cnt")
            W(g, "CC", "cc_cnt")
            DMA(g, "LD", cnta_sb[:, :], cnt_out.ap(), "ld_cnta")

    es_e.close()
    es_psE.close()
    es_f = ExitStack()
    Cf = es_f.enter_context
    y_sbt = C(nc.sbuf_tensor("y_sbt", [128, DC * NTOK], F32))
    G_sb = Cf(nc.sbuf_tensor("G_sb", [128, IC * NTOK], BF16))
    GU_sb = Cf(nc.sbuf_tensor("GU_sb", [128, IC * NTOK], BF16))
    wgu_sb = Cf(nc.sbuf_tensor("wgu_sb", [128, 2 * 2048], BF16))
    wd_sb = Cf(nc.sbuf_tensor("wd_sb", [128, 2 * IC * 128], BF16))
    es_psF = ExitStack()
    gu_ps = [es_psF.enter_context(nc.psum_tensor(f"gu_ps{i}", [128, NTOK], F32)) for i in range(2)]
    d_ps = [es_psF.enter_context(nc.psum_tensor(f"d_ps{i}", [128, NTOK], F32)) for i in range(2)]

    def flkind(fl):
        return "G" if (fl // IC) % 2 == 0 else "U"

    with nc.Block() as blk:
        @blk.gpsimd
        def _(g):
            W(g, "MM", "mm_E")
            W(g, "VE", "ve_E")
            W(g, "SE", "se_E")
            if DEBUG:
                W(g, "LX", "lx_dbgD")
                W(g, "LX", "lx_dbgE")
            for e in range(E):
                for pi, wsrc in ((0, wg), (1, wu)):
                    for it in range(IC):
                        fl = (e * 2 + pi) * IC + it
                        if fl >= 2:
                            W(g, "MM", f"mm_gu_{fl - 2}")
                        DMA(g, "LD", wgu_sb[:, (fl % 2) * 2048:(fl % 2 + 1) * 2048],
                            wsrc[e, it], f"ld_gu_{fl}")
                for ot in range(DC):
                    k = e * DC + ot
                    if k >= 2:
                        W(g, "MM", f"mm_d_{k - 2}")
                    DMA(g, "LD", wd_sb[:, (k % 2) * IC * 128:(k % 2 + 1) * IC * 128],
                        wd[e, ot], f"ld_d_{k}")

        @blk.tensor
        def _(t):
            W(t, "VE", "ve_h1")
            W(t, "SE", "se_E")
            for e in range(E):
                for pi in range(2):
                    for it in range(IC):
                        fl = (e * 2 + pi) * IC + it
                        W(t, "LD", f"ld_gu_{fl}")
                        if fl >= 2:
                            pfl = fl - 2
                            if flkind(pfl) == "G":
                                W(t, "SE", f"se_gfl_{pfl}")
                            else:
                                W(t, "VE", f"ve_gufl_{pfl}")
                        if pi == 0 and it == 0 and e >= 1:
                            W(t, "MM", f"mm_d_{e * DC - 1}")
                        for c in range(DC):
                            for n in range(2):
                                ins = t.matmul(gu_ps[fl % 2][:, n * 512:(n + 1) * 512],
                                               wgu_sb[:, (fl % 2) * 2048 + c * 128:(fl % 2) * 2048 + (c + 1) * 128],
                                               h1_sb[:, c * NTOK + n * 512:c * NTOK + (n + 1) * 512],
                                               start=(c == 0), stop=(c == DC - 1))
                        TI(ins, "MM", f"mm_gu_{fl}")
                for ot in range(DC):
                    k = e * DC + ot
                    W(t, "LD", f"ld_d_{k}")
                    W(t, "VE", f"ve_gufl_{(e * 2 + 1) * IC + IC - 1}")
                    if k >= 2:
                        W(t, "VE", f"ve_y_{k - 2}")
                    for c in range(IC):
                        for n in range(2):
                            ins = t.matmul(d_ps[k % 2][:, n * 512:(n + 1) * 512],
                                           wd_sb[:, (k % 2) * IC * 128 + c * 128:(k % 2) * IC * 128 + (c + 1) * 128],
                                           G_sb[:, c * NTOK + n * 512:c * NTOK + (n + 1) * 512],
                                           start=(c == 0), stop=(c == IC - 1))
                    TI(ins, "MM", f"mm_d_{k}")
            mark("mm_F", "MM")

        @blk.scalar
        def _(s):
            for e in range(E):
                for it in range(IC):
                    fl = (e * 2) * IC + it
                    W(s, "MM", f"mm_gu_{fl}")
                    if e >= 1 and it == 0:
                        W(s, "MM", f"mm_d_{e * DC - 1}")
                    ins = s.activation(G_sb[:, it * NTOK:(it + 1) * NTOK],
                                       gu_ps[fl % 2][:, :], AF.Silu)
                    TI(ins, "SE", f"se_gfl_{fl}")
            mark("se_F", "SE")

        @blk.vector
        def _(v):
            for e in range(E):
                for it in range(IC):
                    flg = (e * 2) * IC + it
                    flu = (e * 2 + 1) * IC + it
                    W(v, "MM", f"mm_gu_{flu}")
                    W(v, "SE", f"se_gfl_{flg}")
                    if it == 0:
                        W(v, "SE", f"se_cb_{e}")
                    v.tensor_tensor(GU_sb[:, it * NTOK:(it + 1) * NTOK],
                                    gu_ps[flu % 2][:, :], G_sb[:, it * NTOK:(it + 1) * NTOK], ALU.mult)
                    ins = v.tensor_tensor(G_sb[:, it * NTOK:(it + 1) * NTOK],
                                          GU_sb[:, it * NTOK:(it + 1) * NTOK],
                                          combbc_sb[:, e * NTOK:(e + 1) * NTOK], ALU.mult)
                    TI(ins, "VE", f"ve_gufl_{flu}")
                    if flu == (e * 2 + 1) * IC + IC - 1:
                        pass
                for ot in range(DC):
                    k = e * DC + ot
                    W(v, "MM", f"mm_d_{k}")
                    if e == 0:
                        ins = v.tensor_copy(y_sbt[:, ot * NTOK:(ot + 1) * NTOK], d_ps[k % 2][:, :])
                    else:
                        ins = v.tensor_tensor(y_sbt[:, ot * NTOK:(ot + 1) * NTOK],
                                              d_ps[k % 2][:, :],
                                              y_sbt[:, ot * NTOK:(ot + 1) * NTOK], ALU.add)
                    TI(ins, "VE", f"ve_y_{k}")
            mark("ve_F", "VE")

    es_f.close()
    es_psF.close()
    es_g = ExitStack()
    Cg = es_g.enter_context
    r2b_sb = Cg(nc.sbuf_tensor("r2b_sb", [128, 2 * NTOK], BF16))
    sq2_sb = Cg(nc.sbuf_tensor("sq2_sb", [128, 2 * NTOK], BF16))
    rows_sb = Cg(nc.sbuf_tensor("rows_sb", [128, NTOK], F32))
    nmcol_sb = Cg(nc.sbuf_tensor("nmcol_sb", [128, 8], F32))
    rscol_sb = Cg(nc.sbuf_tensor("rscol_sb", [128, 8], F32))
    out_sb = Cg(nc.sbuf_tensor("out_sb", [128, 2 * D], F32))
    r_mean2 = Cg(nc.sbuf_tensor("r_mean2", [1, NTOK], F32))
    r_var2 = Cg(nc.sbuf_tensor("r_var2", [1, NTOK], F32))
    r_tmp2 = Cg(nc.sbuf_tensor("r_tmp2", [1, NTOK], F32))
    es_psG1 = ExitStack()
    sums2_ps = es_psG1.enter_context(nc.psum_tensor("sums2_ps", [1, NTOK], F32))
    sqs2_ps = es_psG1.enter_context(nc.psum_tensor("sqs2_ps", [1, NTOK], F32))
    rt_ps = es_psG1.enter_context(nc.psum_tensor("rt_ps", [128, 128], F32))

    with nc.Block() as blk:
        if DEBUG:
            @blk.sync
            def _(sy):
                W(sy, "VE", "ve_F")
                DMA(sy, "LX", dbg_y, y_sbt[:, :], "lx_dbgy")
                W(sy, "VE", "ve_h1")
                DMA(sy, "LX", dbg_h1, h1_sb[:, :], None)
                W(sy, "VE", "ve_combT")
                DMA(sy, "LX", dbg_comb, combbc_sb[0:1, :], "lx_dbgG")

        @blk.vector
        def _(v):
            W(v, "MM", "mm_F")
            if DEBUG:
                W(v, "LX", "lx_dbgy")
            for ct in range(DC):
                v.tensor_tensor(y_sbt[:, ct * NTOK:(ct + 1) * NTOK],
                                y_sbt[:, ct * NTOK:(ct + 1) * NTOK],
                                h1_sb[:, ct * NTOK:(ct + 1) * NTOK], ALU.add)
                if ct >= 2:
                    W(v, "MM", f"mm_s2_{ct - 2}")
                    W(v, "SE", f"se_sq2_{ct - 2}")
                ins = v.tensor_copy(r2b_sb[:, (ct % 2) * NTOK:(ct % 2 + 1) * NTOK],
                                    y_sbt[:, ct * NTOK:(ct + 1) * NTOK])
                TI(ins, "VE", f"ve_r2_{ct}")
            W(v, "MM", "mm_lnsums2")
            FENCE(v, v.tensor_scalar_mul(r_mean2[0:1, :], sums2_ps[0:1, :], 1.0 / D))
            FENCE(v, v.tensor_scalar_mul(r_var2[0:1, :], sqs2_ps[0:1, :], 1.0 / D))
            FENCE(v, v.tensor_tensor(r_tmp2[0:1, :], r_mean2[0:1, :], r_mean2[0:1, :], ALU.mult))
            v.memset(rows_sb[:, :], 0.0)
            ins = v.tensor_tensor(r_var2[0:1, :], r_var2[0:1, :], r_tmp2[0:1, :], ALU.subtract)
            TI(ins, "VE", "ve_var2")
            W(v, "SE", "se_std2")
            FENCE(v, v.reciprocal(rows_sb[32:33, :], r_tmp2[0:1, :]))
            ins = v.tensor_scalar_mul(rows_sb[0:1, :], r_mean2[0:1, :], -1.0)
            TI(ins, "VE", "ve_rows2")
            for tt in range(8):
                W(v, "MM", f"mm_rt_{tt}")
                FENCE(v, v.tensor_copy(nmcol_sb[:, tt:tt + 1], rt_ps[:, 0:1]))
                ins = v.tensor_copy(rscol_sb[:, tt:tt + 1], rt_ps[:, 32:33])
                TI(ins, "VE", f"ve_cols_{tt}")

        @blk.scalar
        def _(s):
            for ct in range(DC):
                W(s, "VE", f"ve_r2_{ct}")
                if ct >= 2:
                    W(s, "MM", f"mm_sq2_{ct - 2}")
                ins = s.square(sq2_sb[:, (ct % 2) * NTOK:(ct % 2 + 1) * NTOK],
                               r2b_sb[:, (ct % 2) * NTOK:(ct % 2 + 1) * NTOK])
                TI(ins, "SE", f"se_sq2_{ct}")
            W(s, "VE", "ve_var2")
            ins = s.activation(r_tmp2[0:1, :], r_var2[0:1, :], AF.Sqrt,
                               bias=constf_sb[0:1, 0:1], scale=1.0)
            TI(ins, "SE", "se_std2")

        @blk.tensor
        def _(t):
            for ct in range(DC):
                W(t, "VE", f"ve_r2_{ct}")
                for n in range(2):
                    ins = t.matmul(sums2_ps[0:1, n * 512:(n + 1) * 512], onesc_sb[:, 0:1],
                                   r2b_sb[:, (ct % 2) * NTOK + n * 512:(ct % 2) * NTOK + (n + 1) * 512],
                                   start=(ct == 0), stop=(ct == DC - 1))
                TI(ins, "MM", f"mm_s2_{ct}")
                W(t, "SE", f"se_sq2_{ct}")
                for n in range(2):
                    ins = t.matmul(sqs2_ps[0:1, n * 512:(n + 1) * 512], onesc_sb[:, 0:1],
                                   sq2_sb[:, (ct % 2) * NTOK + n * 512:(ct % 2) * NTOK + (n + 1) * 512],
                                   start=(ct == 0), stop=(ct == DC - 1))
                TI(ins, "MM", f"mm_sq2_{ct}")
            mark("mm_lnsums2", "MM")
            W(t, "VE", "ve_rows2")
            for tt in range(8):
                if tt >= 1:
                    W(t, "VE", f"ve_cols_{tt - 1}")
                ins = t.transpose(rt_ps[:, :], rows_sb[:, tt * 128:(tt + 1) * 128], id_f[:, :])
                TI(ins, "MM", f"mm_rt_{tt}")

    es_psG1.close()
    es_psG2 = ExitStack()
    oT_ps = [es_psG2.enter_context(nc.psum_tensor(f"oT_ps{i}", [128, D], F32)) for i in range(2)]

    with nc.Block() as blk:
        @blk.tensor
        def _(t):
            W(t, "VE", "ve_cols_7")
            for tt in range(8):
                if tt >= 2:
                    W(t, "VE", f"ve_out_{tt - 2}")
                for ct in range(DC):
                    ins = t.transpose(oT_ps[tt % 2][:, ct * 128:(ct + 1) * 128],
                                      y_sbt[:, ct * NTOK + tt * 128:ct * NTOK + (tt + 1) * 128],
                                      id_f[:, :])
                TI(ins, "MM", f"mm_oT_{tt}")

        @blk.vector
        def _(v):
            for tt in range(8):
                W(v, "MM", f"mm_oT_{tt}")
                W(v, "VE", f"ve_cols_{tt}")
                if tt >= 2:
                    W(v, "OUT", f"out_{tt - 2}")
                ins = v.tensor_scalar(out_sb[:, (tt % 2) * D:(tt % 2 + 1) * D],
                                      oT_ps[tt % 2][:, :],
                                      nmcol_sb[:, tt:tt + 1], rscol_sb[:, tt:tt + 1],
                                      ALU.add, ALU.mult)
                TI(ins, "VE", f"ve_out_{tt}")

        @blk.sync
        def _(sy):
            for tt in range(8):
                W(sy, "VE", f"ve_out_{tt}")
                DMA(sy, "OUT", y_out[tt * 128:(tt + 1) * 128, :],
                    out_sb[:, (tt % 2) * D:(tt % 2 + 1) * D], f"out_{tt}")
            W(sy, "OUT", "out_7")

    es_psG2.close()
    es_g.close()
    es.close()
    return nc, rec


def _prep_in_maps(L0_attn_x, Wq, Wk, Wv, Wo, Wr, Wg, Wu, Wd):
    x = np.ascontiguousarray(np.asarray(L0_attn_x, dtype=np.float32))
    Wq = np.asarray(Wq, dtype=np.float32)
    Wk = np.asarray(Wk, dtype=np.float32)
    Wv = np.asarray(Wv, dtype=np.float32)
    Wo = np.asarray(Wo, dtype=np.float32)
    Wr = np.asarray(Wr, dtype=np.float32)
    Wg = np.asarray(Wg, dtype=np.float32)
    Wu = np.asarray(Wu, dtype=np.float32)
    Wd = np.asarray(Wd, dtype=np.float32)

    def col_tiles(Wmat, nout_t, nin_t):
        r = Wmat.reshape(nout_t, 128, nin_t, 128).transpose(0, 3, 2, 1)
        return np.ascontiguousarray(r.reshape(nout_t, 128, nin_t * 128))

    wq_t = col_tiles(Wq, DC, DC)
    wk_t = col_tiles(Wk, DC, DC)
    wv_t = col_tiles(Wv, DC, DC).reshape(DC, 128, DC, 128)
    wo_t = col_tiles(Wo, DC, DC)
    wv4_all = []
    for hg in range(2):
        sl = wv_t[hg * 8:(hg + 1) * 8]
        r = sl.reshape(2, 4, 128, DC, 128).transpose(0, 2, 3, 1, 4)
        wv4_all.append(np.ascontiguousarray(r.reshape(2, 128, DC * 512)))
    wr_t = np.ascontiguousarray(Wr.reshape(8, DC, 128).transpose(2, 1, 0).reshape(128, DC * 8))
    wg_t = np.ascontiguousarray(
        Wg.reshape(E, IC, 128, DC, 128).transpose(0, 1, 4, 3, 2).reshape(E, IC, 128, DC * 128))
    wu_t = np.ascontiguousarray(
        Wu.reshape(E, IC, 128, DC, 128).transpose(0, 1, 4, 3, 2).reshape(E, IC, 128, DC * 128))
    wd_t = np.ascontiguousarray(
        Wd.reshape(E, DC, 128, IC, 128).transpose(0, 1, 4, 3, 2).reshape(E, DC, 128, IC * 128))

    idbf = np.eye(128, dtype=ml_dtypes.bfloat16)
    idf32 = np.eye(128, dtype=np.float32)
    maskd = np.triu(np.full((128, 128), NEG, dtype=np.float32), k=1)
    onesr = np.ones((1, 128), dtype=ml_dtypes.bfloat16)
    onescb = np.ones((128, 1), dtype=ml_dtypes.bfloat16)
    seld = np.zeros((8, E * 128), dtype=ml_dtypes.bfloat16)
    for e in range(E):
        seld[e, e * 128:(e + 1) * 128] = 1
    alld = np.zeros((64, 8), dtype=np.float32)
    for r_ in range(8):
        for e_ in range(8):
            alld[8 * r_ + e_, e_] = 1.0

    in_maps = []
    xT_by_b = [np.ascontiguousarray(x[b].T) for b in range(B)]
    for c in range(8):
        b, hg = c // 2, c % 2
        prefd_c = np.zeros((64, 8), dtype=np.float32)
        for r_ in range(c):
            for e_ in range(8):
                prefd_c[8 * r_ + e_, e_] = 1.0
        zidx = np.zeros((128, DC), dtype=np.int32)
        for ctk in range(DC):
            zidx[:, ctk] = (ctk // 8) * 2048 + hg * 1024 + (ctk % 8) * 128 + np.arange(128)
        in_maps.append({
            "xT": xT_by_b[b],
            "xTo": np.ascontiguousarray(xT_by_b[b][:, hg * NTOK:(hg + 1) * NTOK]),
            "wq": np.ascontiguousarray(wq_t[hg * 8:(hg + 1) * 8]),
            "wk": np.ascontiguousarray(wk_t[hg * 8:(hg + 1) * 8]),
            "wv4": wv4_all[hg],
            "wo": wo_t,
            "wr": wr_t,
            "wg": wg_t,
            "wu": wu_t,
            "wd": wd_t,
            "idbf": idbf,
            "idf32": idf32,
            "maskd": maskd,
            "onesr": onesr,
            "onescb": onescb,
            "seld": seld,
            "prefd": prefd_c,
            "alld": alld,
            "zidxd": zidx,
            "constf": np.array([[EPS, 0.0, 0.0, 0.0]], dtype=np.float32),
        })
    return in_maps


_NC_CACHE = None


def _get_nc():
    global _NC_CACHE
    if _NC_CACHE is None:
        _, rec = build_program({}, False)
        nc, _ = build_program(rec, True)
        _NC_CACHE = nc
    return _NC_CACHE


def kernel(L0_attn_x, L0_attn_Wq, L0_attn_Wk, L0_attn_Wv, L0_attn_Wo,
           L0_ffn_Wr, L0_ffn_Wg, L0_ffn_Wu, L0_ffn_Wd):
    nc = _get_nc()
    in_maps = _prep_in_maps(L0_attn_x, L0_attn_Wq, L0_attn_Wk, L0_attn_Wv, L0_attn_Wo,
                            L0_ffn_Wr, L0_ffn_Wg, L0_ffn_Wu, L0_ffn_Wd)
    res = run_bass_kernel_spmd(nc, in_maps, core_ids=list(range(8)))
    out = np.empty((B, T, D), dtype=np.float32)
    for c in range(8):
        b, hg = c // 2, c % 2
        out[b, hg * NTOK:(hg + 1) * NTOK, :] = res.results[c]["y_out"]
    return out


# version 02
# speedup vs baseline: 1.0030x; 1.0030x over previous
import numpy as np
import ml_dtypes
import concourse.bass as bass
import concourse.mybir as mybir
from concourse.bass import IndirectOffsetOnAxis
from concourse.bass_utils import run_bass_kernel_spmd
from contextlib import ExitStack

F32 = mybir.dt.float32
BF16 = mybir.dt.bfloat16
I32 = mybir.dt.int32
AF = mybir.ActivationFunctionType
ALU = mybir.AluOpType
AX = mybir.AxisListType.X

B, T, D = 4, 2048, 2048
HD = 128
II = 1408
E = 8
NTOK = 1024
DC = D // 128
IC = II // 128
QT = T // 128
SCALE = float(1.0 / np.float64(np.sqrt(np.float32(HD))))
NEG = -3.0e38
CAP = 2560.0
EPS = 1e-5
DEBUG = False

SEM_NAMES = ("LD", "LX", "MM", "VE", "SE", "CC", "OUT")


def build_program(plan, final):
    nc = bass.Bass("TRN2", target_bir_lowering=False)
    es = ExitStack()
    C = es.enter_context
    SEM = {n: C(nc.semaphore("s" + n)) for n in SEM_NAMES}
    rec = {}
    cnt = {n: 0 for n in SEM_NAMES}

    def inc(s, key=None, by=1):
        cnt[s] += by
        if key is not None:
            assert key not in rec, key
            rec[key] = cnt[s]
        return cnt[s]

    def mark(key, s):
        rec[key] = cnt[s]

    def W(eng, s, key):
        if final:
            assert key in plan, f"missing milestone {key}"
        v = plan.get(key, 0)
        if v > 0:
            eng.wait_ge(SEM[s], v)

    def DMA(eng, s, out, in_, key=None):
        eng.dma_start(out=out, in_=in_).then_inc(SEM[s], 16)
        inc(s, key, 16)

    def TI(ins, s, key=None):
        ins.then_inc(SEM[s], 1)
        return inc(s, key, 1)

    def FENCE(eng, ins):
        ins.then_inc(SEM["VE"], 1)
        c = inc("VE", None, 1)
        eng.wait_ge(SEM["VE"], c)

    xT = nc.dram_tensor("xT", [D, T], F32, kind="ExternalInput").ap()
    xTo = nc.dram_tensor("xTo", [D, NTOK], F32, kind="ExternalInput").ap()
    wq = nc.dram_tensor("wq", [8, 128, DC * 128], F32, kind="ExternalInput").ap()
    wk = nc.dram_tensor("wk", [8, 128, DC * 128], F32, kind="ExternalInput").ap()
    wv4 = nc.dram_tensor("wv4", [2, 128, DC * 512], F32, kind="ExternalInput").ap()
    wo = nc.dram_tensor("wo", [DC, 128, DC * 128], F32, kind="ExternalInput").ap()
    wr = nc.dram_tensor("wr", [128, DC * 8], F32, kind="ExternalInput").ap()
    wg = nc.dram_tensor("wg", [E, IC, 128, DC * 128], F32, kind="ExternalInput").ap()
    wu = nc.dram_tensor("wu", [E, IC, 128, DC * 128], F32, kind="ExternalInput").ap()
    wd = nc.dram_tensor("wd", [E, DC, 128, IC * 128], F32, kind="ExternalInput").ap()
    idbf = nc.dram_tensor("idbf", [128, 128], BF16, kind="ExternalInput").ap()
    idf32 = nc.dram_tensor("idf32", [128, 128], F32, kind="ExternalInput").ap()
    maskd = nc.dram_tensor("maskd", [128, 128], F32, kind="ExternalInput").ap()
    onesr = nc.dram_tensor("onesr", [1, 128], BF16, kind="ExternalInput").ap()
    onescb = nc.dram_tensor("onescb", [128, 1], BF16, kind="ExternalInput").ap()
    seld = nc.dram_tensor("seld", [8, E * 128], BF16, kind="ExternalInput").ap()
    prefd = nc.dram_tensor("prefd", [64, 8], F32, kind="ExternalInput").ap()
    alld = nc.dram_tensor("alld", [64, 8], F32, kind="ExternalInput").ap()
    zidxd = nc.dram_tensor("zidxd", [128, DC], I32, kind="ExternalInput").ap()
    constf = nc.dram_tensor("constf", [1, 4], F32, kind="ExternalInput").ap()
    y_out = nc.dram_tensor("y_out", [NTOK, D], F32, kind="ExternalOutput").ap()
    zc1_in = nc.dram_tensor("zc1_in", [1536, NTOK], BF16)
    zc1_out = nc.dram_tensor("zc1_out", [3072, NTOK], BF16)
    zc2_in = nc.dram_tensor("zc2_in", [512, NTOK], BF16)
    zc2_out = nc.dram_tensor("zc2_out", [1024, NTOK], BF16)
    cnt_in = nc.dram_tensor("cnt_in", [8, 2], F32)
    cnt_out = nc.dram_tensor("cnt_out", [64, 2], F32)
    if DEBUG:
        dbg_res1 = nc.dram_tensor("dbg_res1", [128, DC * NTOK], BF16, kind="ExternalOutput").ap()
        dbg_h1 = nc.dram_tensor("dbg_h1", [128, DC * NTOK], BF16, kind="ExternalOutput").ap()
        dbg_comb = nc.dram_tensor("dbg_comb", [1, E * NTOK], BF16, kind="ExternalOutput").ap()
        dbg_y = nc.dram_tensor("dbg_y", [128, DC * NTOK], F32, kind="ExternalOutput").ap()
        dbg_gates = nc.dram_tensor("dbg_gates", [128, 64], F32, kind="ExternalOutput").ap()
        dbg_z = nc.dram_tensor("dbg_z", [128, 8 * T], BF16, kind="ExternalOutput").ap()
        dbg_zcat = nc.dram_tensor("dbg_zcat", [128, DC * NTOK], BF16, kind="ExternalOutput").ap()
        dbg_rtA = nc.dram_tensor("dbg_rtA", [8, 4 * NTOK], F32, kind="ExternalOutput").ap()
        dbg_rtB = nc.dram_tensor("dbg_rtB", [8, 4 * NTOK], F32, kind="ExternalOutput").ap()
        dbg_off = nc.dram_tensor("dbg_off", [8, 4], F32, kind="ExternalOutput").ap()
        dbg_cnta = nc.dram_tensor("dbg_cnta", [64, 2], F32, kind="ExternalOutput").ap()

    id_bf = C(nc.sbuf_tensor("id_bf", [128, 128], BF16))
    id_f = C(nc.sbuf_tensor("id_f", [128, 128], F32))
    mask_sb = C(nc.sbuf_tensor("mask_sb", [128, 128], F32))
    onesr_sb = C(nc.sbuf_tensor("onesr_sb", [1, 128], BF16))
    onesc_sb = C(nc.sbuf_tensor("onesc_sb", [128, 1], BF16))
    sel_sb = C(nc.sbuf_tensor("sel_sb", [8, E * 128], BF16))
    pref_sb = C(nc.sbuf_tensor("pref_sb", [64, 8], F32))
    all_sb = C(nc.sbuf_tensor("all_sb", [64, 8], F32))
    wr_sb = C(nc.sbuf_tensor("wr_sb", [128, DC * 8], BF16))
    zidx_sb = C(nc.sbuf_tensor("zidx_sb", [128, DC], I32))
    constf_sb = C(nc.sbuf_tensor("constf_sb", [1, 4], F32))

    es_att = ExitStack()
    Ca = es_att.enter_context
    K_sb = Ca(nc.sbuf_tensor("K_sb", [128, 8 * T], BF16))
    Q_sb = Ca(nc.sbuf_tensor("Q_sb", [128, 8 * T], BF16))
    V_sb = Ca(nc.sbuf_tensor("V_sb", [128, 2 * 16 * 512], BF16))
    es_qkv = ExitStack()
    Cq = es_qkv.enter_context
    xT_bf = Cq(nc.sbuf_tensor("xT_bf", [128, DC * T], BF16))
    wqk_sb = Cq(nc.sbuf_tensor("wqk_sb", [128, 2 * 2048], BF16))
    wv4_sb = Cq(nc.sbuf_tensor("wv4_sb", [128, 2 * DC * 512], BF16))
    es_psB = ExitStack()
    qk_ps = es_psB.enter_context(nc.psum_tensor("qk_ps", [128, 1024], F32))
    v_ps = es_psB.enter_context(nc.psum_tensor("v_ps", [128, 1024], F32))

    with nc.Block() as blk:
        @blk.sync
        def _(sy):
            for src, dst in ((idbf, id_bf), (idf32, id_f), (maskd, mask_sb),
                             (onesr, onesr_sb), (onescb, onesc_sb), (seld, sel_sb),
                             (prefd, pref_sb), (alld, all_sb), (zidxd, zidx_sb), (constf, constf_sb)):
                DMA(sy, "LX", dst[:, :], src)
            mark("lx_consts", "LX")

        @blk.gpsimd
        def _(g):
            DMA(g, "LD", wr_sb[:, :], wr, "ld_wr")

    with nc.Block() as blk:
        @blk.gpsimd
        def _(g):
            for c in range(DC):
                DMA(g, "LD", xT_bf[:, c * T:(c + 1) * T], xT[c * 128:(c + 1) * 128, :])
            mark("ld_x", "LD")
            for h4 in range(2):
                DMA(g, "LD", wv4_sb[:, h4 * DC * 512:(h4 + 1) * DC * 512], wv4[h4])
            mark("ld_wv4", "LD")
            for k in range(16):
                if k >= 2:
                    W(g, "MM", f"mm_qkw_{k - 2}")
                DMA(g, "LD", wqk_sb[:, (k % 2) * 2048:(k % 2 + 1) * 2048],
                    (wq if k % 2 == 0 else wk)[k // 2], f"ld_qkw_{k}")

        @blk.tensor
        def _(t):
            W(t, "LD", "ld_x")
            for k in range(16):
                W(t, "LD", f"ld_qkw_{k}")
                for m in range(4):
                    pass
                for npair in range(2):
                    gidx = 2 * k + npair
                    if gidx >= 1:
                        W(t, "SE", f"se_qk_{gidx - 1}")
                    for c in range(DC):
                        for ni in range(2):
                            n = 2 * npair + ni
                            ins = t.matmul(qk_ps[:, ni * 512:(ni + 1) * 512],
                                           wqk_sb[:, (k % 2) * 2048 + c * 128:(k % 2) * 2048 + (c + 1) * 128],
                                           xT_bf[:, c * T + n * 512:c * T + (n + 1) * 512],
                                           start=(c == 0), stop=(c == DC - 1))
                    TI(ins, "MM", f"mm_qk_{gidx}")
                mark(f"mm_qkw_{k}", "MM")
            W(t, "LD", "ld_wv4")
            for j in range(32):
                h4, st = j // 16, j % 16
                if j >= 2:
                    W(t, "VE", f"ve_v_{j - 2}")
                for c in range(DC):
                    ins = t.matmul(v_ps[:, (j % 2) * 512:(j % 2 + 1) * 512],
                                   xT_bf[:, c * T + st * 128:c * T + (st + 1) * 128],
                                   wv4_sb[:, h4 * DC * 512 + c * 512:h4 * DC * 512 + (c + 1) * 512],
                                   start=(c == 0), stop=(c == DC - 1))
                TI(ins, "MM", f"mm_v_{j}")
            mark("mm_B", "MM")

        @blk.scalar
        def _(s):
            for k in range(16):
                dest = Q_sb if k % 2 == 0 else K_sb
                h = k // 2
                for npair in range(2):
                    gidx = 2 * k + npair
                    W(s, "MM", f"mm_qk_{gidx}")
                    if k % 2 == 0:
                        ins = s.activation(dest[:, h * T + npair * 1024:h * T + (npair + 1) * 1024],
                                           qk_ps[:, :], AF.Copy, bias=0.0, scale=SCALE)
                    else:
                        ins = s.copy(dest[:, h * T + npair * 1024:h * T + (npair + 1) * 1024],
                                     qk_ps[:, :])
                    TI(ins, "SE", f"se_qk_{gidx}")
            mark("se_B", "SE")

        @blk.vector
        def _(v):
            for j in range(32):
                W(v, "MM", f"mm_v_{j}")
                ins = v.tensor_copy(V_sb[:, j * 512:(j + 1) * 512],
                                    v_ps[:, (j % 2) * 512:(j % 2 + 1) * 512])
                TI(ins, "VE", f"ve_v_{j}")
            mark("ve_B", "VE")

    es_qkv.close()
    es_psB.close()

    wgu_sb = C(nc.sbuf_tensor("wgu_sb", [128, 2 * 2048], BF16, side="right"))
    wd_sb = C(nc.sbuf_tensor("wd_sb", [128, 2 * IC * 128], BF16, side="right"))
    es_z = ExitStack()
    Z_sb = es_z.enter_context(nc.sbuf_tensor("Z_sb", [128, 8 * T], BF16, side="right"))
    es_attc = ExitStack()
    Cc = es_attc.enter_context
    pf_sb = Cc(nc.sbuf_tensor("pf_sb", [128, 2 * 2048], F32))
    pb_sb = Cc(nc.sbuf_tensor("pb_sb", [128, 2 * 2048], BF16))
    ptr_sb = Cc(nc.sbuf_tensor("ptr_sb", [128, 2 * 1024], BF16))
    nm_sb = Cc(nc.sbuf_tensor("nm_sb", [128, 4], F32))
    scr_sb = Cc(nc.sbuf_tensor("scr_sb", [128, 4], F32))
    es_psC = ExitStack()
    s_ps = es_psC.enter_context(nc.psum_tensor("s_ps", [128, 2048], F32))
    ptr_ps = [es_psC.enter_context(nc.psum_tensor(f"ptr_ps{i}", [128, 1024], BF16)) for i in range(2)]
    z_ps = [es_psC.enter_context(nc.psum_tensor(f"z_ps{i}", [128, 512], F32)) for i in range(2)]

    iters = [(h, qt) for h in range(8) for qt in range(QT)]
    NIT = len(iters)

    def grps(vis):
        out = [(0, min(8, vis))]
        if vis > 8:
            out.append((8, vis - 8))
        return out

    with nc.Block() as blk:
        @blk.tensor
        def _(t):
            W(t, "SE", "se_B")
            W(t, "VE", "ve_B")
            for step in range(NIT + 1):
                if step < NIT:
                    h, qt = iters[step]
                    vis = qt + 1
                    if step >= 1:
                        W(t, "SE", f"se_exp_{step - 1}")
                    nw = (vis * 128 + 511) // 512
                    for n in range(nw):
                        wdt = min(512, vis * 128 - n * 512)
                        ins = t.matmul(s_ps[:, n * 512:n * 512 + wdt],
                                       Q_sb[:, h * T + qt * 128:h * T + (qt + 1) * 128],
                                       K_sb[:, h * T + n * 512:h * T + n * 512 + wdt],
                                       start=True, stop=True)
                    TI(ins, "MM", f"mm_s_{step}")
                if step >= 1:
                    p = step - 1
                    ph, pqt = iters[p]
                    pvis = pqt + 1
                    h4p, hhp = ph // 4, ph % 4
                    W(t, "SE", f"se_pcast_{p}")
                    for gi, (o, gl) in enumerate(grps(pvis)):
                        if p >= 1:
                            W(t, "VE", f"ve_ptrev_{p - 1}_{gi}")
                        for sc in range(o, o + gl):
                            ins = t.transpose(ptr_ps[gi][:, (sc - o) * 128:(sc - o + 1) * 128],
                                              pb_sb[:, (p % 2) * 2048 + sc * 128:(p % 2) * 2048 + (sc + 1) * 128],
                                              id_bf[:, :])
                        TI(ins, "MM", f"mm_ptr_{p}_{gi}")
                    zg = p // 4
                    if pqt % 4 == 0 and zg >= 2:
                        W(t, "VE", f"ve_zev_{zg - 2}")
                    for gi, (o, gl) in enumerate(grps(pvis)):
                        W(t, "VE", f"ve_ptrevc_{p}_{gi}")
                        for sc in range(o, o + gl):
                            ins = t.matmul(z_ps[zg % 2][:, (pqt % 4) * 128:(pqt % 4 + 1) * 128],
                                           V_sb[:, h4p * 16 * 512 + sc * 512 + hhp * 128:
                                                h4p * 16 * 512 + sc * 512 + (hhp + 1) * 128],
                                           ptr_sb[:, gi * 1024 + (sc - o) * 128:gi * 1024 + (sc - o + 1) * 128],
                                           start=(sc == 0), stop=(sc == pvis - 1))
                    TI(ins, "MM", f"mm_pv_{p}")
            mark("mm_C", "MM")

        @blk.vector
        def _(v):
            for step in range(NIT + 1):
                if step < NIT:
                    h, qt = iters[step]
                    vis = qt + 1
                    W(v, "MM", f"mm_s_{step}")
                    FENCE(v, v.tensor_tensor(s_ps[:, (vis - 1) * 128:vis * 128],
                                              s_ps[:, (vis - 1) * 128:vis * 128], mask_sb[:, :], ALU.add))
                    ins = v.reduce_max(nm_sb[:, 0:1], s_ps[:, 0:vis * 128], AX, negate=True)
                    TI(ins, "VE", f"ve_stats_{step}")
                if step >= 1:
                    p = step - 1
                    gg = grps(iters[p][1] + 1)
                    W(v, "MM", f"mm_ptr_{p}_0")
                    if p >= 1:
                        W(v, "MM", f"mm_pv_{p - 1}")
                    o, gl = gg[0]
                    ins = v.tensor_copy(ptr_sb[:, 0:gl * 128], ptr_ps[0][:, 0:gl * 128])
                    TI(ins, "VE", f"ve_ptrevc_{p}_0")
                    mark(f"ve_ptrev_{p}_0", "VE")
                if step < NIT:
                    h, qt = iters[step]
                    vis = qt + 1
                    W(v, "SE", f"se_exp_{step}")
                    FENCE(v, v.reduce_sum(nm_sb[:, 2:3],
                                          pf_sb[:, (step % 2) * 2048:(step % 2) * 2048 + vis * 128], AX))
                    ins = v.reciprocal(nm_sb[:, 3:4], nm_sb[:, 2:3])
                    TI(ins, "VE", f"ve_recip_{step}")
                if step >= 1:
                    p = step - 1
                    ph, pqt = iters[p]
                    gg = grps(pqt + 1)
                    if len(gg) > 1:
                        W(v, "MM", f"mm_ptr_{p}_1")
                        o, gl = gg[1]
                        ins = v.tensor_copy(ptr_sb[:, 1024:1024 + gl * 128], ptr_ps[1][:, 0:gl * 128])
                        TI(ins, "VE", f"ve_ptrevc_{p}_1")
                        mark(f"ve_ptrev_{p}_1", "VE")
                    else:
                        mark(f"ve_ptrevc_{p}_1", "VE")
                        mark(f"ve_ptrev_{p}_1", "VE")
                    if pqt % 4 == 3:
                        W(v, "MM", f"mm_pv_{p}")
                        ins = v.tensor_copy(Z_sb[:, ph * T + (pqt - 3) * 128:ph * T + (pqt + 1) * 128],
                                            z_ps[(p // 4) % 2][:, :])
                        TI(ins, "VE", f"ve_zev_{p // 4}")
            mark("ve_C", "VE")

        @blk.scalar
        def _(s):
            for step in range(NIT):
                h, qt = iters[step]
                vis = qt + 1
                W(s, "VE", f"ve_stats_{step}")
                ins = s.activation(pf_sb[:, (step % 2) * 2048:(step % 2) * 2048 + vis * 128],
                                   s_ps[:, 0:vis * 128], AF.Exp,
                                   bias=nm_sb[:, 0:1], scale=1.0)
                TI(ins, "SE", f"se_exp_{step}")
                W(s, "VE", f"ve_recip_{step}")
                if step >= 2:
                    ngr = len(grps(iters[step - 2][1] + 1))
                    W(s, "MM", f"mm_ptr_{step - 2}_{ngr - 1}")
                ins = s.activation(pb_sb[:, (step % 2) * 2048:(step % 2) * 2048 + vis * 128],
                                   pf_sb[:, (step % 2) * 2048:(step % 2) * 2048 + vis * 128],
                                   AF.Copy, bias=0.0, scale=nm_sb[:, 3:4])
                TI(ins, "SE", f"se_pcast_{step}")
            mark("se_C", "SE")

    es_attc.close()
    es_psC.close()
    es_att.close()

    es_d = ExitStack()
    Cd = es_d.enter_context
    h1_sb = C(nc.sbuf_tensor("h1_sb", [128, DC * NTOK], BF16))
    Zcat = Cd(nc.sbuf_tensor("Zcat", [128, DC * NTOK], BF16))
    xo_sb = Cd(nc.sbuf_tensor("xo_sb", [128, 4 * NTOK], F32))
    res1_sb = Cd(nc.sbuf_tensor("res1_sb", [128, DC * NTOK], BF16))
    wo_sb = Cd(nc.sbuf_tensor("wo_sb", [128, 2 * 2048], BF16))
    sq_sb = Cd(nc.sbuf_tensor("sq_sb", [128, 2 * NTOK], BF16))
    r_mean = Cd(nc.sbuf_tensor("r_mean", [1, NTOK], F32))
    r_var = Cd(nc.sbuf_tensor("r_var", [1, NTOK], F32))
    pack = Cd(nc.sbuf_tensor("pack", [1, 2 * NTOK], F32))
    pack_bf = Cd(nc.sbuf_tensor("pack_bf", [1, 2 * NTOK], BF16))
    tmp_f = Cd(nc.sbuf_tensor("tmp_f", [128, 2 * NTOK], F32))
    es_ps_s = ExitStack()
    sums_ps = es_ps_s.enter_context(nc.psum_tensor("sums_ps", [1, NTOK], F32))
    sqs_ps = es_ps_s.enter_context(nc.psum_tensor("sqs_ps", [1, NTOK], F32))
    es_ps_at = ExitStack()
    at_ps = [es_ps_at.enter_context(nc.psum_tensor(f"at_ps{i}", [128, NTOK], F32)) for i in range(2)]

    with nc.Block() as blk:
        @blk.sync
        def _(sy):
            for h in range(8):
                W(sy, "VE", f"ve_zev_{h * 4 + 3}")
                for half in range(2):
                    if h <= 5:
                        dst = zc1_in[h * 256 + half * 128:h * 256 + (half + 1) * 128, :]
                    else:
                        dst = zc2_in[(h - 6) * 256 + half * 128:(h - 6) * 256 + (half + 1) * 128, :]
                    DMA(sy, "LX", dst,
                        Z_sb[:, h * T + half * NTOK:h * T + half * NTOK + NTOK])
                if h == 5:
                    mark("lx_zstage1", "LX")
            mark("lx_zstage", "LX")
            if DEBUG:
                DMA(sy, "LX", dbg_z, Z_sb[:, :])
            W(sy, "MM", "mm_C")
            W(sy, "SE", "se_C")
            for ct in range(DC):
                if ct >= 4:
                    W(sy, "VE", f"ve_res1_{ct - 4}")
                DMA(sy, "LX", xo_sb[:, (ct % 4) * NTOK:(ct % 4 + 1) * NTOK],
                    xTo[ct * 128:(ct + 1) * 128, :], f"lx_xo_{ct}")
            if DEBUG:
                W(sy, "LD", "ld_zcat")
                DMA(sy, "LX", dbg_zcat, Zcat[:, :])
                W(sy, "VE", "ve_res1_all")
                DMA(sy, "LX", dbg_res1, res1_sb[:, :], "lx_dbgD")

        @blk.gpsimd
        def _(g):
            W(g, "MM", "mm_C")
            W(g, "SE", "se_C")
            W(g, "LX", "lx_zstage1")
            g.collective_compute("AllGather", ALU.bypass,
                                 replica_groups=[[0, 1], [2, 3], [4, 5], [6, 7]],
                                 ins=[zc1_in.ap().opt()], outs=[zc1_out.ap().opt()]).then_inc(SEM["CC"], 1)
            inc("CC", "cc_z1")
            W(g, "LX", "lx_zstage")
            g.collective_compute("AllGather", ALU.bypass,
                                 replica_groups=[[0, 1], [2, 3], [4, 5], [6, 7]],
                                 ins=[zc2_in.ap().opt()], outs=[zc2_out.ap().opt()]).then_inc(SEM["CC"], 1)
            inc("CC", "cc_z2")
            for ot in range(2):
                DMA(g, "LD", wo_sb[:, (ot % 2) * 2048:(ot % 2 + 1) * 2048], wo[ot], f"ld_wo_{ot}")
            for fl in range(2):
                DMA(g, "LD", wgu_sb[:, (fl % 2) * 2048:(fl % 2 + 1) * 2048],
                    wg[0, fl], f"ld_gu_{fl}")
            W(g, "CC", "cc_z1")
            for ctk in range(DC):
                if ctk % 8 == 6:
                    break
                g.indirect_dma_start(
                    out=Zcat[:, ctk * NTOK:(ctk + 1) * NTOK],
                    out_offset=None,
                    in_=zc1_out.ap(),
                    in_offset=IndirectOffsetOnAxis(ap=zidx_sb[:, ctk:ctk + 1], axis=0),
                ).then_inc(SEM["LD"], 16)
                inc("LD", None, 16)
            for ctk in range(DC):
                if ctk % 8 <= 5 and ctk != 7 and ctk != 6 and ctk < 8:
                    continue
                pass
            for ctk in [8, 9, 10, 11, 12, 13]:
                g.indirect_dma_start(
                    out=Zcat[:, ctk * NTOK:(ctk + 1) * NTOK],
                    out_offset=None,
                    in_=zc1_out.ap(),
                    in_offset=IndirectOffsetOnAxis(ap=zidx_sb[:, ctk:ctk + 1], axis=0),
                ).then_inc(SEM["LD"], 16)
                inc("LD", None, 16)
            W(g, "CC", "cc_z2")
            for ctk in [6, 7, 14, 15]:
                g.indirect_dma_start(
                    out=Zcat[:, ctk * NTOK:(ctk + 1) * NTOK],
                    out_offset=None,
                    in_=zc2_out.ap(),
                    in_offset=IndirectOffsetOnAxis(ap=zidx_sb[:, ctk:ctk + 1], axis=0),
                ).then_inc(SEM["LD"], 16)
                inc("LD", None, 16)
            mark("ld_zcat", "LD")
            for ot in range(2, DC):
                W(g, "MM", f"mm_at_{ot - 2}")
                DMA(g, "LD", wo_sb[:, (ot % 2) * 2048:(ot % 2 + 1) * 2048], wo[ot], f"ld_wo_{ot}")

        @blk.tensor
        def _(t):
            W(t, "LD", "ld_zcat")
            for ot in range(DC):
                W(t, "LD", f"ld_wo_{ot}")
                if ot >= 2:
                    W(t, "VE", f"ve_res1_{ot - 2}")
                for c in range(DC):
                    for n in range(2):
                        ins = t.matmul(at_ps[ot % 2][:, n * 512:(n + 1) * 512],
                                       wo_sb[:, (ot % 2) * 2048 + c * 128:(ot % 2) * 2048 + (c + 1) * 128],
                                       Zcat[:, c * NTOK + n * 512:c * NTOK + (n + 1) * 512],
                                       start=(c == 0), stop=(c == DC - 1))
                TI(ins, "MM", f"mm_at_{ot}")
            for po in range(DC):
                W(t, "VE", f"ve_res1_{po}")
                for n in range(2):
                    ins = t.matmul(sums_ps[0:1, n * 512:(n + 1) * 512], onesc_sb[:, 0:1],
                                   res1_sb[:, po * NTOK + n * 512:po * NTOK + (n + 1) * 512],
                                   start=(po == 0), stop=(po == DC - 1))
                TI(ins, "MM", f"mm_sum_{po}")
                W(t, "SE", f"se_sq_{po}")
                for n in range(2):
                    ins = t.matmul(sqs_ps[0:1, n * 512:(n + 1) * 512], onesc_sb[:, 0:1],
                                   sq_sb[:, (po % 2) * NTOK + n * 512:(po % 2) * NTOK + (n + 1) * 512],
                                   start=(po == 0), stop=(po == DC - 1))
                TI(ins, "MM", f"mm_sq_{po}")
            mark("mm_lnsums", "MM")

        @blk.vector
        def _(v):
            for ot in range(DC):
                W(v, "MM", f"mm_at_{ot}")
                W(v, "LX", f"lx_xo_{ot}")
                ins = v.tensor_tensor(res1_sb[:, ot * NTOK:(ot + 1) * NTOK],
                                      at_ps[ot % 2][:, :],
                                      xo_sb[:, (ot % 4) * NTOK:(ot % 4 + 1) * NTOK], ALU.add)
                TI(ins, "VE", f"ve_res1_{ot}")
            mark("ve_res1_all", "VE")

        @blk.scalar
        def _(s):
            for ot in range(DC):
                W(s, "VE", f"ve_res1_{ot}")
                if ot >= 2:
                    W(s, "MM", f"mm_sq_{ot - 2}")
                ins = s.square(sq_sb[:, (ot % 2) * NTOK:(ot % 2 + 1) * NTOK],
                               res1_sb[:, ot * NTOK:(ot + 1) * NTOK])
                TI(ins, "SE", f"se_sq_{ot}")

    es_ps_at.close()
    es_ps_ab = ExitStack()
    ab_ps = es_ps_ab.enter_context(nc.psum_tensor("ab_ps", [128, 2048], F32))

    with nc.Block() as blk:
        @blk.vector
        def _(v):
            W(v, "MM", "mm_lnsums")
            FENCE(v, v.tensor_scalar_mul(r_mean[0:1, :], sums_ps[0:1, :], 1.0 / D))
            FENCE(v, v.tensor_scalar_mul(r_var[0:1, :], sqs_ps[0:1, :], 1.0 / D))
            FENCE(v, v.tensor_tensor(pack[0:1, 0:NTOK], r_mean[0:1, :], r_mean[0:1, :], ALU.mult))
            ins = v.tensor_tensor(r_var[0:1, :], r_var[0:1, :], pack[0:1, 0:NTOK], ALU.subtract)
            TI(ins, "VE", "ve_var1")
            W(v, "SE", "se_std1")
            FENCE(v, v.reciprocal(pack[0:1, 0:NTOK], pack[0:1, NTOK:2 * NTOK]))
            FENCE(v, v.scalar_tensor_tensor(pack[0:1, NTOK:2 * NTOK], r_mean[0:1, :], -1.0,
                                            pack[0:1, 0:NTOK], ALU.mult, ALU.mult))
            ins = v.tensor_copy(pack_bf[0:1, :], pack[0:1, :])
            TI(ins, "VE", "ve_lnrows1")
            W(v, "MM", "mm_ab1")
            for ct in range(DC):
                v.tensor_tensor(tmp_f[:, (ct % 2) * NTOK:(ct % 2 + 1) * NTOK],
                                res1_sb[:, ct * NTOK:(ct + 1) * NTOK],
                                ab_ps[:, 0:NTOK], ALU.mult)
                ins = v.tensor_tensor(h1_sb[:, ct * NTOK:(ct + 1) * NTOK],
                                      tmp_f[:, (ct % 2) * NTOK:(ct % 2 + 1) * NTOK],
                                      ab_ps[:, NTOK:2 * NTOK], ALU.add)
                TI(ins, "VE", f"ve_h1_{ct}")
            mark("ve_h1", "VE")

        @blk.scalar
        def _(s):
            W(s, "VE", "ve_var1")
            ins = s.activation(pack[0:1, NTOK:2 * NTOK], r_var[0:1, :], AF.Sqrt,
                               bias=constf_sb[0:1, 0:1], scale=1.0)
            TI(ins, "SE", "se_std1")

        @blk.tensor
        def _(t):
            W(t, "VE", "ve_lnrows1")
            for n in range(4):
                ins = t.matmul(ab_ps[:, n * 512:(n + 1) * 512], onesr_sb[0:1, :],
                               pack_bf[0:1, n * 512:(n + 1) * 512], start=True, stop=True)
            TI(ins, "MM", "mm_ab1")

    es_d.close()
    es_z.close()
    es_ps_ab.close()
    es_ps_s.close()
    combbc_sb = C(nc.sbuf_tensor("combbc_sb", [128, E * NTOK], BF16))
    es_e = ExitStack()
    Ce = es_e.enter_context
    gates_sb = Ce(nc.sbuf_tensor("gates_sb", [128, 64], F32))
    oh1_sb = Ce(nc.sbuf_tensor("oh1_sb", [128, 64], BF16))
    oh2_sb = Ce(nc.sbuf_tensor("oh2_sb", [128, 64], BF16))
    v1_sb = Ce(nc.sbuf_tensor("v1_sb", [128, 64], BF16))
    v2_sb = Ce(nc.sbuf_tensor("v2_sb", [128, 64], BF16))
    sc8_sb = Ce(nc.sbuf_tensor("sc8_sb", [128, 8], F32))
    gm_sb = Ce(nc.sbuf_tensor("gm_sb", [128, 8], F32))
    rtA = Ce(nc.sbuf_tensor("rtA", [8, 4 * NTOK], F32))
    rtB = Ce(nc.sbuf_tensor("rtB", [8, 4 * NTOK], F32))
    combT_bf = Ce(nc.sbuf_tensor("combT_bf", [8, NTOK], BF16))
    cnt_sb = Ce(nc.sbuf_tensor("cnt_sb", [8, 2], F32))
    cnta_sb = Ce(nc.sbuf_tensor("cnta_sb", [64, 2], F32))
    off_sb = Ce(nc.sbuf_tensor("off_sb", [8, 4], F32))
    es_psE = ExitStack()
    lg_ps = es_psE.enter_context(nc.psum_tensor("lg_ps", [128, 64], F32))
    trE_ps = [es_psE.enter_context(nc.psum_tensor(f"trE_ps{i}", [8, 512], F32)) for i in range(2)]
    pmA_ps = es_psE.enter_context(nc.psum_tensor("pmA_ps", [8, 2], F32))
    pmB_ps = es_psE.enter_context(nc.psum_tensor("pmB_ps", [8, 2], F32))
    cb_ps = es_psE.enter_context(nc.psum_tensor("cb_ps", [128, NTOK], F32))

    with nc.Block() as blk:
        @blk.tensor
        def _(t):
            W(t, "VE", "ve_h1")
            for tt in range(8):
                for c in range(DC):
                    ins = t.matmul(lg_ps[:, tt * 8:(tt + 1) * 8],
                                   h1_sb[:, c * NTOK + tt * 128:c * NTOK + (tt + 1) * 128],
                                   wr_sb[:, c * 8:(c + 1) * 8],
                                   start=(c == 0), stop=(c == DC - 1))
            TI(ins, "MM", "mm_lg_all")
            W(t, "VE", "ve_top2all")
            for tt in range(8):
                if tt >= 2:
                    W(t, "VE", f"ve_trev_{tt - 2}")
                for ki, srcT in enumerate((oh1_sb, oh2_sb, v1_sb, v2_sb)):
                    ins = t.matmul(trE_ps[tt % 2][0:8, ki * 128:(ki + 1) * 128],
                                   srcT[:, tt * 8:(tt + 1) * 8], id_bf[:, :],
                                   start=True, stop=True)
                TI(ins, "MM", f"mm_tr_{tt}")
            W(t, "LD", "ld_cnta")
            t.matmul(pmA_ps[0:8, 0:2], pref_sb[:, :], cnta_sb[:, :], start=True, stop=True)
            ins = t.matmul(pmB_ps[0:8, 0:2], all_sb[:, :], cnta_sb[:, :], start=True, stop=True)
            TI(ins, "MM", "mm_pref")
            for e in range(E):
                W(t, "VE", "ve_combT")
                if e >= 1:
                    W(t, "SE", f"se_cb_{e - 1}")
                for n in range(2):
                    ins = t.matmul(cb_ps[:, n * 512:(n + 1) * 512],
                                   sel_sb[:, e * 128:(e + 1) * 128],
                                   combT_bf[0:8, n * 512:(n + 1) * 512],
                                   start=True, stop=True)
                TI(ins, "MM", f"mm_cb_{e}")
            mark("mm_E", "MM")

        @blk.vector
        def _(v):
            W(v, "MM", "mm_lg_all")
            ins = v.reduce_max(sc8_sb[:, 0:1], lg_ps[:, :], AX, negate=True)
            TI(ins, "VE", "ve_lgm")
            W(v, "SE", "se_gexp")
            for tt in range(8):
                g8 = gates_sb[:, tt * 8:(tt + 1) * 8]
                FENCE(v, v.reduce_sum(sc8_sb[:, 1:2], g8, AX))
                FENCE(v, v.reciprocal(sc8_sb[:, 2:3], sc8_sb[:, 1:2]))
                FENCE(v, v.tensor_scalar_mul(g8, g8, sc8_sb[:, 2:3]))
                FENCE(v, v.reduce_max(sc8_sb[:, 3:4], g8, AX))
                FENCE(v, v.tensor_single_scalar(oh1_sb[:, tt * 8:(tt + 1) * 8], g8, sc8_sb[:, 3:4], ALU.is_equal))
                FENCE(v, v.tensor_tensor(gm_sb[:, :], g8, oh1_sb[:, tt * 8:(tt + 1) * 8], ALU.mult))
                FENCE(v, v.tensor_tensor(gm_sb[:, :], g8, gm_sb[:, :], ALU.subtract))
                FENCE(v, v.reduce_max(sc8_sb[:, 4:5], gm_sb[:, :], AX))
                FENCE(v, v.tensor_single_scalar(oh2_sb[:, tt * 8:(tt + 1) * 8], gm_sb[:, :], sc8_sb[:, 4:5], ALU.is_equal))
                FENCE(v, v.tensor_single_scalar(v1_sb[:, tt * 8:(tt + 1) * 8], oh1_sb[:, tt * 8:(tt + 1) * 8],
                                                sc8_sb[:, 3:4], ALU.mult))
                FENCE(v, v.tensor_single_scalar(v2_sb[:, tt * 8:(tt + 1) * 8], oh2_sb[:, tt * 8:(tt + 1) * 8],
                                                sc8_sb[:, 4:5], ALU.mult))
            ins = v.reduce_sum(sc8_sb[:, 5:6], h1_sb[:, 0:NTOK], AX)
            TI(ins, "VE", "ve_top2all")
            for tt in range(8):
                W(v, "MM", f"mm_tr_{tt}")
                for ki in range(4):
                    ins = v.tensor_copy(rtA[0:8, ki * NTOK + tt * 128:ki * NTOK + (tt + 1) * 128],
                                        trE_ps[tt % 2][0:8, ki * 128:(ki + 1) * 128])
                TI(ins, "VE", f"ve_trev_{tt}")
            FENCE(v, v.tensor_tensor_scan(rtB[0:8, 0:NTOK], rtA[0:8, 0:NTOK], rtA[0:8, 0:NTOK],
                                          0.0, ALU.add, ALU.bypass))
            FENCE(v, v.tensor_tensor_scan(rtB[0:8, NTOK:2 * NTOK], rtA[0:8, NTOK:2 * NTOK],
                                          rtA[0:8, NTOK:2 * NTOK], 0.0, ALU.add, ALU.bypass))
            FENCE(v, v.tensor_copy(cnt_sb[0:8, 0:1], rtB[0:8, NTOK - 1:NTOK]))
            FENCE(v, v.tensor_copy(cnt_sb[0:8, 1:2], rtB[0:8, 2 * NTOK - 1:2 * NTOK]))
            ins = v.reduce_sum(sc8_sb[0:8, 6:7], rtB[0:8, 0:NTOK], AX)
            TI(ins, "VE", "ve_cnt")
            W(v, "MM", "mm_pref")
            FENCE(v, v.tensor_copy(off_sb[0:8, 0:2], pmA_ps[0:8, :]))
            FENCE(v, v.tensor_copy(off_sb[0:8, 2:3], pmB_ps[0:8, 0:1]))
            ins = v.tensor_tensor(off_sb[0:8, 3:4], off_sb[0:8, 2:3], off_sb[0:8, 1:2], ALU.add)
            FENCE(v, ins)
            mark("ve_off", "VE")
            FENCE(v, v.tensor_scalar(rtB[0:8, 0:NTOK], rtB[0:8, 0:NTOK], off_sb[0:8, 0:1], CAP,
                                     ALU.add, ALU.is_le))
            FENCE(v, v.tensor_scalar(rtB[0:8, NTOK:2 * NTOK], rtB[0:8, NTOK:2 * NTOK], off_sb[0:8, 3:4], CAP,
                                     ALU.add, ALU.is_le))
            FENCE(v, v.tensor_tensor(rtB[0:8, 2 * NTOK:3 * NTOK], rtA[0:8, 2 * NTOK:3 * NTOK],
                                     rtB[0:8, 0:NTOK], ALU.mult))
            FENCE(v, v.tensor_tensor(rtB[0:8, 3 * NTOK:4 * NTOK], rtA[0:8, 3 * NTOK:4 * NTOK],
                                     rtB[0:8, NTOK:2 * NTOK], ALU.mult))
            FENCE(v, v.tensor_tensor(rtB[0:8, 2 * NTOK:3 * NTOK], rtB[0:8, 2 * NTOK:3 * NTOK],
                                     rtB[0:8, 3 * NTOK:4 * NTOK], ALU.add))
            ins = v.tensor_copy(combT_bf[0:8, :], rtB[0:8, 2 * NTOK:3 * NTOK])
            TI(ins, "VE", "ve_combT")
            mark("ve_E", "VE")

        @blk.scalar
        def _(s):
            W(s, "VE", "ve_lgm")
            s.activation(gates_sb[:, :], lg_ps[:, :],
                         AF.Exp, bias=sc8_sb[:, 0:1], scale=1.0)
            ins = s.copy(gm_sb[:, :], gates_sb[:, 0:8])
            TI(ins, "SE", "se_gexp")
            for e in range(E):
                W(s, "MM", f"mm_cb_{e}")
                ins = s.copy(combbc_sb[:, e * NTOK:(e + 1) * NTOK], cb_ps[:, :])
                TI(ins, "SE", f"se_cb_{e}")
            mark("se_E", "SE")

        if DEBUG:
            @blk.sync
            def _(sy):
                W(sy, "VE", "ve_combT")
                W(sy, "LD", "ld_cnta")
                DMA(sy, "LX", dbg_gates, gates_sb[:, :])
                DMA(sy, "LX", dbg_rtA, rtA[0:8, :])
                DMA(sy, "LX", dbg_rtB, rtB[0:8, :])
                DMA(sy, "LX", dbg_off, off_sb[0:8, :])
                DMA(sy, "LX", dbg_cnta, cnta_sb[:, :], "lx_dbgE")

        @blk.gpsimd
        def _(g):
            W(g, "VE", "ve_cnt")
            DMA(g, "LD", cnt_in.ap(), cnt_sb[0:8, :], "ld_cntin")
            W(g, "LD", "ld_cntin")
            g.collective_compute("AllGather", ALU.bypass,
                                 replica_groups=[[0, 1, 2, 3, 4, 5, 6, 7]],
                                 ins=[cnt_in.ap().opt()], outs=[cnt_out.ap().opt()]).then_inc(SEM["CC"], 1)
            inc("CC", "cc_cnt")
            W(g, "CC", "cc_cnt")
            DMA(g, "LD", cnta_sb[:, :], cnt_out.ap(), "ld_cnta")

    es_e.close()
    es_psE.close()
    es_f = ExitStack()
    Cf = es_f.enter_context
    y_sbt = C(nc.sbuf_tensor("y_sbt", [128, DC * NTOK], F32))
    G_sb = Cf(nc.sbuf_tensor("G_sb", [128, IC * NTOK], BF16))
    GU_sb = Cf(nc.sbuf_tensor("GU_sb", [128, IC * NTOK], BF16))
    es_psF = ExitStack()
    gu_ps = [es_psF.enter_context(nc.psum_tensor(f"gu_ps{i}", [128, NTOK], F32)) for i in range(2)]
    d_ps = [es_psF.enter_context(nc.psum_tensor(f"d_ps{i}", [128, NTOK], F32)) for i in range(2)]

    def flkind(fl):
        return "G" if (fl // IC) % 2 == 0 else "U"

    with nc.Block() as blk:
        @blk.gpsimd
        def _(g):
            if DEBUG:
                W(g, "LX", "lx_dbgD")
                W(g, "LX", "lx_dbgE")
            for e in range(E):
                for pi, wsrc in ((0, wg), (1, wu)):
                    for it in range(IC):
                        fl = (e * 2 + pi) * IC + it
                        if fl < 2:
                            continue
                        W(g, "MM", f"mm_gu_{fl - 2}")
                        DMA(g, "LD", wgu_sb[:, (fl % 2) * 2048:(fl % 2 + 1) * 2048],
                            wsrc[e, it], f"ld_gu_{fl}")
                for ot in range(DC):
                    k = e * DC + ot
                    if k >= 2:
                        W(g, "MM", f"mm_d_{k - 2}")
                    DMA(g, "LD", wd_sb[:, (k % 2) * IC * 128:(k % 2 + 1) * IC * 128],
                        wd[e, ot], f"ld_d_{k}")

        @blk.tensor
        def _(t):
            W(t, "VE", "ve_h1")
            W(t, "SE", "se_E")
            W(t, "VE", "ve_combT")
            for e in range(E):
                for pi in range(2):
                    for it in range(IC):
                        fl = (e * 2 + pi) * IC + it
                        W(t, "LD", f"ld_gu_{fl}")
                        if fl >= 2:
                            pfl = fl - 2
                            if flkind(pfl) == "G":
                                W(t, "SE", f"se_gfl_{pfl}")
                            else:
                                W(t, "VE", f"ve_gufl_{pfl}")
                        if pi == 0 and it == 0 and e >= 1:
                            W(t, "MM", f"mm_d_{e * DC - 1}")
                        for c in range(DC):
                            for n in range(2):
                                ins = t.matmul(gu_ps[fl % 2][:, n * 512:(n + 1) * 512],
                                               wgu_sb[:, (fl % 2) * 2048 + c * 128:(fl % 2) * 2048 + (c + 1) * 128],
                                               h1_sb[:, c * NTOK + n * 512:c * NTOK + (n + 1) * 512],
                                               start=(c == 0), stop=(c == DC - 1))
                        TI(ins, "MM", f"mm_gu_{fl}")
                for ot in range(DC):
                    k = e * DC + ot
                    W(t, "LD", f"ld_d_{k}")
                    W(t, "VE", f"ve_gufl_{(e * 2 + 1) * IC + IC - 1}")
                    if k >= 2:
                        W(t, "VE", f"ve_y_{k - 2}")
                    for c in range(IC):
                        for n in range(2):
                            ins = t.matmul(d_ps[k % 2][:, n * 512:(n + 1) * 512],
                                           wd_sb[:, (k % 2) * IC * 128 + c * 128:(k % 2) * IC * 128 + (c + 1) * 128],
                                           G_sb[:, c * NTOK + n * 512:c * NTOK + (n + 1) * 512],
                                           start=(c == 0), stop=(c == IC - 1))
                    TI(ins, "MM", f"mm_d_{k}")
            mark("mm_F", "MM")

        @blk.scalar
        def _(s):
            for e in range(E):
                for it in range(IC):
                    fl = (e * 2) * IC + it
                    W(s, "MM", f"mm_gu_{fl}")
                    if e >= 1 and it == 0:
                        W(s, "MM", f"mm_d_{e * DC - 1}")
                    ins = s.activation(G_sb[:, it * NTOK:(it + 1) * NTOK],
                                       gu_ps[fl % 2][:, :], AF.Silu)
                    TI(ins, "SE", f"se_gfl_{fl}")
            mark("se_F", "SE")

        @blk.vector
        def _(v):
            for e in range(E):
                for it in range(IC):
                    flg = (e * 2) * IC + it
                    flu = (e * 2 + 1) * IC + it
                    W(v, "MM", f"mm_gu_{flu}")
                    W(v, "SE", f"se_gfl_{flg}")
                    if it == 0:
                        W(v, "SE", f"se_cb_{e}")
                    v.tensor_tensor(GU_sb[:, it * NTOK:(it + 1) * NTOK],
                                    gu_ps[flu % 2][:, :], G_sb[:, it * NTOK:(it + 1) * NTOK], ALU.mult)
                    ins = v.tensor_tensor(G_sb[:, it * NTOK:(it + 1) * NTOK],
                                          GU_sb[:, it * NTOK:(it + 1) * NTOK],
                                          combbc_sb[:, e * NTOK:(e + 1) * NTOK], ALU.mult)
                    TI(ins, "VE", f"ve_gufl_{flu}")
                    if flu == (e * 2 + 1) * IC + IC - 1:
                        pass
                for ot in range(DC):
                    k = e * DC + ot
                    W(v, "MM", f"mm_d_{k}")
                    if e == 0:
                        ins = v.tensor_copy(y_sbt[:, ot * NTOK:(ot + 1) * NTOK], d_ps[k % 2][:, :])
                    else:
                        ins = v.tensor_tensor(y_sbt[:, ot * NTOK:(ot + 1) * NTOK],
                                              d_ps[k % 2][:, :],
                                              y_sbt[:, ot * NTOK:(ot + 1) * NTOK], ALU.add)
                    TI(ins, "VE", f"ve_y_{k}")
            mark("ve_F", "VE")

    es_f.close()
    es_psF.close()
    es_g = ExitStack()
    Cg = es_g.enter_context
    r2b_sb = Cg(nc.sbuf_tensor("r2b_sb", [128, 2 * NTOK], BF16))
    sq2_sb = Cg(nc.sbuf_tensor("sq2_sb", [128, 2 * NTOK], BF16))
    rows_sb = Cg(nc.sbuf_tensor("rows_sb", [128, NTOK], F32))
    nmcol_sb = Cg(nc.sbuf_tensor("nmcol_sb", [128, 8], F32))
    rscol_sb = Cg(nc.sbuf_tensor("rscol_sb", [128, 8], F32))
    out_sb = Cg(nc.sbuf_tensor("out_sb", [128, 2 * D], F32))
    r_mean2 = Cg(nc.sbuf_tensor("r_mean2", [1, NTOK], F32))
    r_var2 = Cg(nc.sbuf_tensor("r_var2", [1, NTOK], F32))
    r_tmp2 = Cg(nc.sbuf_tensor("r_tmp2", [1, NTOK], F32))
    es_psG1 = ExitStack()
    sums2_ps = es_psG1.enter_context(nc.psum_tensor("sums2_ps", [1, NTOK], F32))
    sqs2_ps = es_psG1.enter_context(nc.psum_tensor("sqs2_ps", [1, NTOK], F32))
    rt_ps = es_psG1.enter_context(nc.psum_tensor("rt_ps", [128, 128], F32))

    with nc.Block() as blk:
        if DEBUG:
            @blk.sync
            def _(sy):
                W(sy, "VE", "ve_F")
                DMA(sy, "LX", dbg_y, y_sbt[:, :], "lx_dbgy")
                W(sy, "VE", "ve_h1")
                DMA(sy, "LX", dbg_h1, h1_sb[:, :], None)
                W(sy, "VE", "ve_combT")
                DMA(sy, "LX", dbg_comb, combbc_sb[0:1, :], "lx_dbgG")

        @blk.vector
        def _(v):
            W(v, "MM", "mm_F")
            if DEBUG:
                W(v, "LX", "lx_dbgy")
            for ct in range(DC):
                v.tensor_tensor(y_sbt[:, ct * NTOK:(ct + 1) * NTOK],
                                y_sbt[:, ct * NTOK:(ct + 1) * NTOK],
                                h1_sb[:, ct * NTOK:(ct + 1) * NTOK], ALU.add)
                if ct >= 2:
                    W(v, "MM", f"mm_s2_{ct - 2}")
                    W(v, "SE", f"se_sq2_{ct - 2}")
                ins = v.tensor_copy(r2b_sb[:, (ct % 2) * NTOK:(ct % 2 + 1) * NTOK],
                                    y_sbt[:, ct * NTOK:(ct + 1) * NTOK])
                TI(ins, "VE", f"ve_r2_{ct}")
            W(v, "MM", "mm_lnsums2")
            FENCE(v, v.tensor_scalar_mul(r_mean2[0:1, :], sums2_ps[0:1, :], 1.0 / D))
            FENCE(v, v.tensor_scalar_mul(r_var2[0:1, :], sqs2_ps[0:1, :], 1.0 / D))
            FENCE(v, v.tensor_tensor(r_tmp2[0:1, :], r_mean2[0:1, :], r_mean2[0:1, :], ALU.mult))
            v.memset(rows_sb[:, :], 0.0)
            ins = v.tensor_tensor(r_var2[0:1, :], r_var2[0:1, :], r_tmp2[0:1, :], ALU.subtract)
            TI(ins, "VE", "ve_var2")
            W(v, "SE", "se_std2")
            FENCE(v, v.reciprocal(rows_sb[32:33, :], r_tmp2[0:1, :]))
            ins = v.tensor_scalar_mul(rows_sb[0:1, :], r_mean2[0:1, :], -1.0)
            TI(ins, "VE", "ve_rows2")
            for tt in range(8):
                W(v, "MM", f"mm_rt_{tt}")
                FENCE(v, v.tensor_copy(nmcol_sb[:, tt:tt + 1], rt_ps[:, 0:1]))
                ins = v.tensor_copy(rscol_sb[:, tt:tt + 1], rt_ps[:, 32:33])
                TI(ins, "VE", f"ve_cols_{tt}")

        @blk.scalar
        def _(s):
            for ct in range(DC):
                W(s, "VE", f"ve_r2_{ct}")
                if ct >= 2:
                    W(s, "MM", f"mm_sq2_{ct - 2}")
                ins = s.square(sq2_sb[:, (ct % 2) * NTOK:(ct % 2 + 1) * NTOK],
                               r2b_sb[:, (ct % 2) * NTOK:(ct % 2 + 1) * NTOK])
                TI(ins, "SE", f"se_sq2_{ct}")
            W(s, "VE", "ve_var2")
            ins = s.activation(r_tmp2[0:1, :], r_var2[0:1, :], AF.Sqrt,
                               bias=constf_sb[0:1, 0:1], scale=1.0)
            TI(ins, "SE", "se_std2")

        @blk.tensor
        def _(t):
            for ct in range(DC):
                W(t, "VE", f"ve_r2_{ct}")
                for n in range(2):
                    ins = t.matmul(sums2_ps[0:1, n * 512:(n + 1) * 512], onesc_sb[:, 0:1],
                                   r2b_sb[:, (ct % 2) * NTOK + n * 512:(ct % 2) * NTOK + (n + 1) * 512],
                                   start=(ct == 0), stop=(ct == DC - 1))
                TI(ins, "MM", f"mm_s2_{ct}")
                W(t, "SE", f"se_sq2_{ct}")
                for n in range(2):
                    ins = t.matmul(sqs2_ps[0:1, n * 512:(n + 1) * 512], onesc_sb[:, 0:1],
                                   sq2_sb[:, (ct % 2) * NTOK + n * 512:(ct % 2) * NTOK + (n + 1) * 512],
                                   start=(ct == 0), stop=(ct == DC - 1))
                TI(ins, "MM", f"mm_sq2_{ct}")
            mark("mm_lnsums2", "MM")
            W(t, "VE", "ve_rows2")
            for tt in range(8):
                if tt >= 1:
                    W(t, "VE", f"ve_cols_{tt - 1}")
                ins = t.transpose(rt_ps[:, :], rows_sb[:, tt * 128:(tt + 1) * 128], id_f[:, :])
                TI(ins, "MM", f"mm_rt_{tt}")

    es_psG1.close()
    es_psG2 = ExitStack()
    oT_ps = [es_psG2.enter_context(nc.psum_tensor(f"oT_ps{i}", [128, D], F32)) for i in range(2)]

    with nc.Block() as blk:
        @blk.tensor
        def _(t):
            W(t, "VE", "ve_cols_7")
            for tt in range(8):
                if tt >= 2:
                    W(t, "VE", f"ve_out_{tt - 2}")
                for ct in range(DC):
                    ins = t.transpose(oT_ps[tt % 2][:, ct * 128:(ct + 1) * 128],
                                      y_sbt[:, ct * NTOK + tt * 128:ct * NTOK + (tt + 1) * 128],
                                      id_f[:, :])
                TI(ins, "MM", f"mm_oT_{tt}")

        @blk.vector
        def _(v):
            for tt in range(8):
                W(v, "MM", f"mm_oT_{tt}")
                W(v, "VE", f"ve_cols_{tt}")
                if tt >= 2:
                    W(v, "OUT", f"out_{tt - 2}")
                ins = v.tensor_scalar(out_sb[:, (tt % 2) * D:(tt % 2 + 1) * D],
                                      oT_ps[tt % 2][:, :],
                                      nmcol_sb[:, tt:tt + 1], rscol_sb[:, tt:tt + 1],
                                      ALU.add, ALU.mult)
                TI(ins, "VE", f"ve_out_{tt}")

        @blk.sync
        def _(sy):
            for tt in range(8):
                W(sy, "VE", f"ve_out_{tt}")
                DMA(sy, "OUT", y_out[tt * 128:(tt + 1) * 128, :],
                    out_sb[:, (tt % 2) * D:(tt % 2 + 1) * D], f"out_{tt}")
            W(sy, "OUT", "out_7")

    es_psG2.close()
    es_g.close()
    es.close()
    return nc, rec


def _prep_in_maps(L0_attn_x, Wq, Wk, Wv, Wo, Wr, Wg, Wu, Wd):
    x = np.ascontiguousarray(np.asarray(L0_attn_x, dtype=np.float32))
    Wq = np.asarray(Wq, dtype=np.float32)
    Wk = np.asarray(Wk, dtype=np.float32)
    Wv = np.asarray(Wv, dtype=np.float32)
    Wo = np.asarray(Wo, dtype=np.float32)
    Wr = np.asarray(Wr, dtype=np.float32)
    Wg = np.asarray(Wg, dtype=np.float32)
    Wu = np.asarray(Wu, dtype=np.float32)
    Wd = np.asarray(Wd, dtype=np.float32)

    def col_tiles(Wmat, nout_t, nin_t):
        r = Wmat.reshape(nout_t, 128, nin_t, 128).transpose(0, 3, 2, 1)
        return np.ascontiguousarray(r.reshape(nout_t, 128, nin_t * 128))

    wq_t = col_tiles(Wq, DC, DC)
    wk_t = col_tiles(Wk, DC, DC)
    wv_t = col_tiles(Wv, DC, DC).reshape(DC, 128, DC, 128)
    wo_t = col_tiles(Wo, DC, DC)
    wv4_all = []
    for hg in range(2):
        sl = wv_t[hg * 8:(hg + 1) * 8]
        r = sl.reshape(2, 4, 128, DC, 128).transpose(0, 2, 3, 1, 4)
        wv4_all.append(np.ascontiguousarray(r.reshape(2, 128, DC * 512)))
    wr_t = np.ascontiguousarray(Wr.reshape(8, DC, 128).transpose(2, 1, 0).reshape(128, DC * 8))
    wg_t = np.ascontiguousarray(
        Wg.reshape(E, IC, 128, DC, 128).transpose(0, 1, 4, 3, 2).reshape(E, IC, 128, DC * 128))
    wu_t = np.ascontiguousarray(
        Wu.reshape(E, IC, 128, DC, 128).transpose(0, 1, 4, 3, 2).reshape(E, IC, 128, DC * 128))
    wd_t = np.ascontiguousarray(
        Wd.reshape(E, DC, 128, IC, 128).transpose(0, 1, 4, 3, 2).reshape(E, DC, 128, IC * 128))

    idbf = np.eye(128, dtype=ml_dtypes.bfloat16)
    idf32 = np.eye(128, dtype=np.float32)
    maskd = np.triu(np.full((128, 128), NEG, dtype=np.float32), k=1)
    onesr = np.ones((1, 128), dtype=ml_dtypes.bfloat16)
    onescb = np.ones((128, 1), dtype=ml_dtypes.bfloat16)
    seld = np.zeros((8, E * 128), dtype=ml_dtypes.bfloat16)
    for e in range(E):
        seld[e, e * 128:(e + 1) * 128] = 1
    alld = np.zeros((64, 8), dtype=np.float32)
    for r_ in range(8):
        for e_ in range(8):
            alld[8 * r_ + e_, e_] = 1.0

    in_maps = []
    xT_by_b = [np.ascontiguousarray(x[b].T) for b in range(B)]
    for c in range(8):
        b, hg = c // 2, c % 2
        prefd_c = np.zeros((64, 8), dtype=np.float32)
        for r_ in range(c):
            for e_ in range(8):
                prefd_c[8 * r_ + e_, e_] = 1.0
        zidx = np.zeros((128, DC), dtype=np.int32)
        for ctk in range(DC):
            rank, h = ctk // 8, ctk % 8
            if h <= 5:
                zidx[:, ctk] = rank * 1536 + h * 256 + hg * 128 + np.arange(128)
            else:
                zidx[:, ctk] = rank * 512 + (h - 6) * 256 + hg * 128 + np.arange(128)
        in_maps.append({
            "xT": xT_by_b[b],
            "xTo": np.ascontiguousarray(xT_by_b[b][:, hg * NTOK:(hg + 1) * NTOK]),
            "wq": np.ascontiguousarray(wq_t[hg * 8:(hg + 1) * 8]),
            "wk": np.ascontiguousarray(wk_t[hg * 8:(hg + 1) * 8]),
            "wv4": wv4_all[hg],
            "wo": wo_t,
            "wr": wr_t,
            "wg": wg_t,
            "wu": wu_t,
            "wd": wd_t,
            "idbf": idbf,
            "idf32": idf32,
            "maskd": maskd,
            "onesr": onesr,
            "onescb": onescb,
            "seld": seld,
            "prefd": prefd_c,
            "alld": alld,
            "zidxd": zidx,
            "constf": np.array([[EPS, 0.0, 0.0, 0.0]], dtype=np.float32),
        })
    return in_maps


_NC_CACHE = None


def _get_nc():
    global _NC_CACHE
    if _NC_CACHE is None:
        _, rec = build_program({}, False)
        nc, _ = build_program(rec, True)
        _NC_CACHE = nc
    return _NC_CACHE


def kernel(L0_attn_x, L0_attn_Wq, L0_attn_Wk, L0_attn_Wv, L0_attn_Wo,
           L0_ffn_Wr, L0_ffn_Wg, L0_ffn_Wu, L0_ffn_Wd):
    nc = _get_nc()
    in_maps = _prep_in_maps(L0_attn_x, L0_attn_Wq, L0_attn_Wk, L0_attn_Wv, L0_attn_Wo,
                            L0_ffn_Wr, L0_ffn_Wg, L0_ffn_Wu, L0_ffn_Wd)
    res = run_bass_kernel_spmd(nc, in_maps, core_ids=list(range(8)))
    out = np.empty((B, T, D), dtype=np.float32)
    for c in range(8):
        b, hg = c // 2, c % 2
        out[b, hg * NTOK:(hg + 1) * NTOK, :] = res.results[c]["y_out"]
    return out


# version 08
# speedup vs baseline: 1.0588x; 1.0556x over previous
import numpy as np
import ml_dtypes
import concourse.bass as bass
import concourse.mybir as mybir
from concourse.bass import IndirectOffsetOnAxis
from concourse.bass_utils import run_bass_kernel_spmd
from contextlib import ExitStack

F32 = mybir.dt.float32
BF16 = mybir.dt.bfloat16
I32 = mybir.dt.int32
AF = mybir.ActivationFunctionType
ALU = mybir.AluOpType
AX = mybir.AxisListType.X

B, T, D = 4, 2048, 2048
HD = 128
II = 1408
E = 8
NTOK = 1024
DC = D // 128
IC = II // 128
QT = T // 128
SCALE = float(1.0 / np.float64(np.sqrt(np.float32(HD))))
NEG = -3.0e38
CAP = 2560.0
EPS = 1e-5
DEBUG = False

SEM_NAMES = ("LD", "LX", "MM", "VE", "SE", "CC", "OUT")


def build_program(plan, final):
    nc = bass.Bass("TRN2", target_bir_lowering=False)
    es = ExitStack()
    C = es.enter_context
    SEM = {n: C(nc.semaphore("s" + n)) for n in SEM_NAMES}
    rec = {}
    cnt = {n: 0 for n in SEM_NAMES}

    def inc(s, key=None, by=1):
        cnt[s] += by
        if key is not None:
            assert key not in rec, key
            rec[key] = cnt[s]
        return cnt[s]

    def mark(key, s):
        rec[key] = cnt[s]

    def W(eng, s, key):
        if final:
            assert key in plan, f"missing milestone {key}"
        v = plan.get(key, 0)
        if v > 0:
            eng.wait_ge(SEM[s], v)

    def DMA(eng, s, out, in_, key=None):
        eng.dma_start(out=out, in_=in_).then_inc(SEM[s], 16)
        inc(s, key, 16)

    def TI(ins, s, key=None):
        ins.then_inc(SEM[s], 1)
        return inc(s, key, 1)

    def FENCE(eng, ins):
        ins.then_inc(SEM["VE"], 1)
        c = inc("VE", None, 1)
        eng.wait_ge(SEM["VE"], c)

    xT = nc.dram_tensor("xT", [D, T], F32, kind="ExternalInput").ap()
    xTo = nc.dram_tensor("xTo", [D, NTOK], F32, kind="ExternalInput").ap()
    wq = nc.dram_tensor("wq", [8, 128, DC * 128], F32, kind="ExternalInput").ap()
    wk = nc.dram_tensor("wk", [8, 128, DC * 128], F32, kind="ExternalInput").ap()
    wv4 = nc.dram_tensor("wv4", [2, 128, DC * 512], F32, kind="ExternalInput").ap()
    wo = nc.dram_tensor("wo", [DC, 128, DC * 128], F32, kind="ExternalInput").ap()
    wr = nc.dram_tensor("wr", [128, DC * 8], F32, kind="ExternalInput").ap()
    wg = nc.dram_tensor("wg", [E, IC, 128, DC * 128], F32, kind="ExternalInput").ap()
    wu = nc.dram_tensor("wu", [E, IC, 128, DC * 128], F32, kind="ExternalInput").ap()
    wd = nc.dram_tensor("wd", [E, DC, 128, IC * 128], F32, kind="ExternalInput").ap()
    idbf = nc.dram_tensor("idbf", [128, 128], BF16, kind="ExternalInput").ap()
    idf32 = nc.dram_tensor("idf32", [128, 128], F32, kind="ExternalInput").ap()
    maskd = nc.dram_tensor("maskd", [128, 128], F32, kind="ExternalInput").ap()
    onesr = nc.dram_tensor("onesr", [1, 128], BF16, kind="ExternalInput").ap()
    onescb = nc.dram_tensor("onescb", [128, 1], BF16, kind="ExternalInput").ap()
    seld = nc.dram_tensor("seld", [8, E * 128], BF16, kind="ExternalInput").ap()
    prefd = nc.dram_tensor("prefd", [64, 8], F32, kind="ExternalInput").ap()
    alld = nc.dram_tensor("alld", [64, 8], F32, kind="ExternalInput").ap()
    zidxd = nc.dram_tensor("zidxd", [128, DC], I32, kind="ExternalInput").ap()
    constf = nc.dram_tensor("constf", [1, 4], F32, kind="ExternalInput").ap()
    y_out = nc.dram_tensor("y_out", [NTOK, D], F32, kind="ExternalOutput").ap()
    zc1_in = nc.dram_tensor("zc1_in", [1536, NTOK], BF16)
    zc1_out = nc.dram_tensor("zc1_out", [3072, NTOK], BF16)
    zc2_in = nc.dram_tensor("zc2_in", [512, NTOK], BF16)
    zc2_out = nc.dram_tensor("zc2_out", [1024, NTOK], BF16)
    cnt_in = nc.dram_tensor("cnt_in", [8, 2], F32)
    cnt_out = nc.dram_tensor("cnt_out", [64, 2], F32)
    if DEBUG:
        dbg_res1 = nc.dram_tensor("dbg_res1", [128, DC * NTOK], BF16, kind="ExternalOutput").ap()
        dbg_h1 = nc.dram_tensor("dbg_h1", [128, DC * NTOK], BF16, kind="ExternalOutput").ap()
        dbg_comb = nc.dram_tensor("dbg_comb", [1, E * NTOK], BF16, kind="ExternalOutput").ap()
        dbg_y = nc.dram_tensor("dbg_y", [128, DC * NTOK], F32, kind="ExternalOutput").ap()
        dbg_gates = nc.dram_tensor("dbg_gates", [128, 64], F32, kind="ExternalOutput").ap()
        dbg_z = nc.dram_tensor("dbg_z", [128, 8 * T], BF16, kind="ExternalOutput").ap()
        dbg_zcat = nc.dram_tensor("dbg_zcat", [128, DC * NTOK], BF16, kind="ExternalOutput").ap()
        dbg_rtA = nc.dram_tensor("dbg_rtA", [8, 4 * NTOK], F32, kind="ExternalOutput").ap()
        dbg_rtB = nc.dram_tensor("dbg_rtB", [8, 4 * NTOK], F32, kind="ExternalOutput").ap()
        dbg_off = nc.dram_tensor("dbg_off", [8, 4], F32, kind="ExternalOutput").ap()
        dbg_cnta = nc.dram_tensor("dbg_cnta", [64, 2], F32, kind="ExternalOutput").ap()

    id_bf = C(nc.sbuf_tensor("id_bf", [128, 128], BF16))
    id_f = C(nc.sbuf_tensor("id_f", [128, 128], F32))
    mask_sb = C(nc.sbuf_tensor("mask_sb", [128, 128], F32))
    onesr_sb = C(nc.sbuf_tensor("onesr_sb", [1, 128], BF16))
    onesc_sb = C(nc.sbuf_tensor("onesc_sb", [128, 1], BF16))
    sel_sb = C(nc.sbuf_tensor("sel_sb", [8, E * 128], BF16))
    pref_sb = C(nc.sbuf_tensor("pref_sb", [64, 8], F32))
    all_sb = C(nc.sbuf_tensor("all_sb", [64, 8], F32))
    wr_sb = C(nc.sbuf_tensor("wr_sb", [128, DC * 8], BF16))
    zidx_sb = C(nc.sbuf_tensor("zidx_sb", [128, DC], I32))
    constf_sb = C(nc.sbuf_tensor("constf_sb", [1, 4], F32))

    es_att = ExitStack()
    Ca = es_att.enter_context
    K_sb = Ca(nc.sbuf_tensor("K_sb", [128, 8 * T], BF16))
    Q_sb = Ca(nc.sbuf_tensor("Q_sb", [128, 8 * T], BF16))
    V_sb = Ca(nc.sbuf_tensor("V_sb", [128, 2 * 16 * 512], BF16))
    es_qkv = ExitStack()
    Cq = es_qkv.enter_context
    xT_bf = Cq(nc.sbuf_tensor("xT_bf", [128, DC * T], BF16))
    wqk_sb = Cq(nc.sbuf_tensor("wqk_sb", [128, 2 * 2048], BF16))
    wv4_sb = Cq(nc.sbuf_tensor("wv4_sb", [128, 2 * DC * 512], BF16))
    es_psB = ExitStack()
    qk_ps = es_psB.enter_context(nc.psum_tensor("qk_ps", [128, 1024], F32))
    v_ps = es_psB.enter_context(nc.psum_tensor("v_ps", [128, 1024], F32))

    with nc.Block() as blk:
        @blk.sync
        def _(sy):
            for src, dst in ((idbf, id_bf), (idf32, id_f), (maskd, mask_sb),
                             (onesr, onesr_sb), (onescb, onesc_sb), (seld, sel_sb),
                             (prefd, pref_sb), (alld, all_sb), (zidxd, zidx_sb), (constf, constf_sb)):
                DMA(sy, "LX", dst[:, :], src)
            mark("lx_consts", "LX")

        @blk.gpsimd
        def _(g):
            DMA(g, "LD", wr_sb[:, :], wr, "ld_wr")

    with nc.Block() as blk:
        @blk.gpsimd
        def _(g):
            for k in range(2):
                DMA(g, "LD", wqk_sb[:, (k % 2) * 2048:(k % 2 + 1) * 2048],
                    (wq if k % 2 == 0 else wk)[k // 2], f"ld_qkw_{k}")
            for c in range(DC):
                DMA(g, "LD", xT_bf[:, c * T:(c + 1) * T], xT[c * 128:(c + 1) * 128, :], f"ld_x_{c}")
            mark("ld_x", "LD")
            for h4 in range(2):
                DMA(g, "LD", wv4_sb[:, h4 * DC * 512:(h4 + 1) * DC * 512], wv4[h4])
            mark("ld_wv4", "LD")
            for k in range(2, 16):
                W(g, "MM", f"mm_qkw_{k - 2}")
                DMA(g, "LD", wqk_sb[:, (k % 2) * 2048:(k % 2 + 1) * 2048],
                    (wq if k % 2 == 0 else wk)[k // 2], f"ld_qkw_{k}")

        @blk.tensor
        def _(t):
            for k in range(16):
                W(t, "LD", f"ld_qkw_{k}")
                for npair in range(2):
                    gidx = 2 * k + npair
                    if gidx >= 1:
                        W(t, "SE", f"se_qk_{gidx - 1}")
                    for c in range(DC):
                        if k == 0 and npair == 0:
                            W(t, "LD", f"ld_x_{c}")
                        for ni in range(2):
                            n = 2 * npair + ni
                            ins = t.matmul(qk_ps[:, ni * 512:(ni + 1) * 512],
                                           wqk_sb[:, (k % 2) * 2048 + c * 128:(k % 2) * 2048 + (c + 1) * 128],
                                           xT_bf[:, c * T + n * 512:c * T + (n + 1) * 512],
                                           start=(c == 0), stop=(c == DC - 1))
                    TI(ins, "MM", f"mm_qk_{gidx}")
                mark(f"mm_qkw_{k}", "MM")
            W(t, "LD", "ld_wv4")
            for j in range(32):
                h4, st = j // 16, j % 16
                if j >= 2:
                    W(t, "VE", f"ve_v_{j - 2}")
                for c in range(DC):
                    ins = t.matmul(v_ps[:, (j % 2) * 512:(j % 2 + 1) * 512],
                                   xT_bf[:, c * T + st * 128:c * T + (st + 1) * 128],
                                   wv4_sb[:, h4 * DC * 512 + c * 512:h4 * DC * 512 + (c + 1) * 512],
                                   start=(c == 0), stop=(c == DC - 1))
                TI(ins, "MM", f"mm_v_{j}")
            mark("mm_B", "MM")

        @blk.scalar
        def _(s):
            for k in range(16):
                dest = Q_sb if k % 2 == 0 else K_sb
                h = k // 2
                for npair in range(2):
                    gidx = 2 * k + npair
                    W(s, "MM", f"mm_qk_{gidx}")
                    if k % 2 == 0:
                        ins = s.activation(dest[:, h * T + npair * 1024:h * T + (npair + 1) * 1024],
                                           qk_ps[:, :], AF.Copy, bias=0.0, scale=SCALE)
                    else:
                        ins = s.copy(dest[:, h * T + npair * 1024:h * T + (npair + 1) * 1024],
                                     qk_ps[:, :])
                    TI(ins, "SE", f"se_qk_{gidx}")
            mark("se_B", "SE")

        @blk.vector
        def _(v):
            for j in range(32):
                W(v, "MM", f"mm_v_{j}")
                ins = v.tensor_copy(V_sb[:, j * 512:(j + 1) * 512],
                                    v_ps[:, (j % 2) * 512:(j % 2 + 1) * 512])
                TI(ins, "VE", f"ve_v_{j}")
            mark("ve_B", "VE")

    es_qkv.close()
    es_psB.close()

    wgu_sb = C(nc.sbuf_tensor("wgu_sb", [128, 2 * 2048], BF16, side="right"))
    wd_sb = C(nc.sbuf_tensor("wd_sb", [128, 2 * IC * 128], BF16, side="right"))
    es_z = ExitStack()
    Z_sb = es_z.enter_context(nc.sbuf_tensor("Z_sb", [128, 8 * T], BF16, side="right"))
    es_attc = ExitStack()
    Cc = es_attc.enter_context
    pf_sb = Cc(nc.sbuf_tensor("pf_sb", [128, 2 * 2048], F32))
    pb_sb = Cc(nc.sbuf_tensor("pb_sb", [128, 2 * 2048], BF16))
    ptr_sb = Cc(nc.sbuf_tensor("ptr_sb", [128, 2 * 1024], BF16))
    nm_sb = Cc(nc.sbuf_tensor("nm_sb", [128, 4], F32))
    scr_sb = Cc(nc.sbuf_tensor("scr_sb", [128, 4], F32))
    es_psC = ExitStack()
    s_ps = [es_psC.enter_context(nc.psum_tensor(f"s_ps{i}", [128, 1024], F32)) for i in range(2)]
    ptr_ps = [es_psC.enter_context(nc.psum_tensor(f"ptr_ps{i}", [128, 1024], BF16)) for i in range(2)]
    z_ps = [es_psC.enter_context(nc.psum_tensor(f"z_ps{i}", [128, 512], F32)) for i in range(2)]

    iters = [(h, qt) for h in range(8) for qt in range(QT)]
    NIT = len(iters)

    def grps(vis):
        out = [(0, min(8, vis))]
        if vis > 8:
            out.append((8, vis - 8))
        return out

    CIDX = {}
    _ci = 0
    for _s, (_h, _qt) in enumerate(iters):
        for _k in range(len(grps(_qt + 1))):
            CIDX[(_s, _k)] = _ci
            _ci += 1

    with nc.Block() as blk:
        @blk.tensor
        def _(t):
            W(t, "SE", "se_B")
            W(t, "VE", "ve_B")
            for step in range(NIT + 1):
                if step < NIT:
                    h, qt = iters[step]
                    vis = qt + 1
                    for k, (off, nt) in enumerate(grps(vis)):
                        ci = CIDX[(step, k)]
                        if ci >= 2:
                            W(t, "SE", f"se_expc_{ci - 2}")
                        for w in range(0, nt * 128, 512):
                            wdt = min(512, nt * 128 - w)
                            ins = t.matmul(s_ps[ci % 2][:, w:w + wdt],
                                           Q_sb[:, h * T + qt * 128:h * T + (qt + 1) * 128],
                                           K_sb[:, h * T + off * 128 + w:h * T + off * 128 + w + wdt],
                                           start=True, stop=True)
                        TI(ins, "MM", f"mm_sc_{ci}")
                if step >= 1:
                    p = step - 1
                    ph, pqt = iters[p]
                    pvis = pqt + 1
                    h4p, hhp = ph // 4, ph % 4
                    W(t, "SE", f"se_pcast_{p}")
                    for gi, (o, gl) in enumerate(grps(pvis)):
                        if p >= 1:
                            W(t, "VE", f"ve_ptrev_{p - 1}_{gi}")
                        for sc in range(o, o + gl):
                            ins = t.transpose(ptr_ps[gi][:, (sc - o) * 128:(sc - o + 1) * 128],
                                              pb_sb[:, (p % 2) * 2048 + sc * 128:(p % 2) * 2048 + (sc + 1) * 128],
                                              id_bf[:, :])
                        TI(ins, "MM", f"mm_ptr_{p}_{gi}")
                    zg = p // 4
                    if pqt % 4 == 0 and zg >= 2:
                        W(t, "VE", f"ve_zev_{zg - 2}")
                    for gi, (o, gl) in enumerate(grps(pvis)):
                        W(t, "VE", f"ve_ptrevc_{p}_{gi}")
                        for sc in range(o, o + gl):
                            ins = t.matmul(z_ps[zg % 2][:, (pqt % 4) * 128:(pqt % 4 + 1) * 128],
                                           V_sb[:, h4p * 16 * 512 + sc * 512 + hhp * 128:
                                                h4p * 16 * 512 + sc * 512 + (hhp + 1) * 128],
                                           ptr_sb[:, gi * 1024 + (sc - o) * 128:gi * 1024 + (sc - o + 1) * 128],
                                           start=(sc == 0), stop=(sc == pvis - 1))
                    TI(ins, "MM", f"mm_pv_{p}")
            mark("mm_C", "MM")

        @blk.vector
        def _(v):
            for step in range(NIT + 1):
                if step < NIT:
                    h, qt = iters[step]
                    vis = qt + 1
                    gg = grps(vis)
                    loff, lnt = gg[-1]
                    lci = CIDX[(step, len(gg) - 1)]
                    W(v, "MM", f"mm_sc_{lci}")
                    ins = v.tensor_tensor(s_ps[lci % 2][:, (lnt - 1) * 128:lnt * 128],
                                          s_ps[lci % 2][:, (lnt - 1) * 128:lnt * 128],
                                          mask_sb[:, :], ALU.add)
                    TI(ins, "VE", f"ve_mask_{step}")
                if step >= 1:
                    p = step - 1
                    gg = grps(iters[p][1] + 1)
                    W(v, "MM", f"mm_ptr_{p}_0")
                    if p >= 1:
                        W(v, "MM", f"mm_pv_{p - 1}")
                    o, gl = gg[0]
                    ins = v.tensor_copy(ptr_sb[:, 0:gl * 128], ptr_ps[0][:, 0:gl * 128])
                    TI(ins, "VE", f"ve_ptrevc_{p}_0")
                    mark(f"ve_ptrev_{p}_0", "VE")
                if step < NIT:
                    h, qt = iters[step]
                    vis = qt + 1
                    lci = CIDX[(step, len(grps(vis)) - 1)]
                    W(v, "SE", f"se_expc_{lci}")
                    FENCE(v, v.reduce_sum(nm_sb[:, 2:3],
                                          pf_sb[:, (step % 2) * 2048:(step % 2) * 2048 + vis * 128], AX))
                    ins = v.reciprocal(nm_sb[:, 3:4], nm_sb[:, 2:3])
                    TI(ins, "VE", f"ve_recip_{step}")
                if step >= 1:
                    p = step - 1
                    ph, pqt = iters[p]
                    gg = grps(pqt + 1)
                    if len(gg) > 1:
                        W(v, "MM", f"mm_ptr_{p}_1")
                        o, gl = gg[1]
                        ins = v.tensor_copy(ptr_sb[:, 1024:1024 + gl * 128], ptr_ps[1][:, 0:gl * 128])
                        TI(ins, "VE", f"ve_ptrevc_{p}_1")
                        mark(f"ve_ptrev_{p}_1", "VE")
                    else:
                        mark(f"ve_ptrevc_{p}_1", "VE")
                        mark(f"ve_ptrev_{p}_1", "VE")
                    if pqt % 4 == 3:
                        W(v, "MM", f"mm_pv_{p}")
                        ins = v.tensor_copy(Z_sb[:, ph * T + (pqt - 3) * 128:ph * T + (pqt + 1) * 128],
                                            z_ps[(p // 4) % 2][:, :])
                        TI(ins, "VE", f"ve_zev_{p // 4}")
            mark("ve_C", "VE")

        @blk.scalar
        def _(s):
            for step in range(NIT):
                h, qt = iters[step]
                vis = qt + 1
                gg = grps(vis)
                for k, (off, nt) in enumerate(gg):
                    ci = CIDX[(step, k)]
                    W(s, "MM", f"mm_sc_{ci}")
                    if k == len(gg) - 1:
                        W(s, "VE", f"ve_mask_{step}")
                    ins = s.activation(pf_sb[:, (step % 2) * 2048 + off * 128:
                                             (step % 2) * 2048 + off * 128 + nt * 128],
                                       s_ps[ci % 2][:, 0:nt * 128], AF.Exp,
                                       bias=0.0, scale=1.0)
                    TI(ins, "SE", f"se_expc_{ci}")
                W(s, "VE", f"ve_recip_{step}")
                if step >= 2:
                    ngr = len(grps(iters[step - 2][1] + 1))
                    W(s, "MM", f"mm_ptr_{step - 2}_{ngr - 1}")
                ins = s.activation(pb_sb[:, (step % 2) * 2048:(step % 2) * 2048 + vis * 128],
                                   pf_sb[:, (step % 2) * 2048:(step % 2) * 2048 + vis * 128],
                                   AF.Copy, bias=0.0, scale=nm_sb[:, 3:4])
                TI(ins, "SE", f"se_pcast_{step}")
            mark("se_C", "SE")

    es_attc.close()
    es_psC.close()
    es_att.close()

    es_d = ExitStack()
    Cd = es_d.enter_context
    h1_sb = C(nc.sbuf_tensor("h1_sb", [128, DC * NTOK], BF16))
    Zcat = Cd(nc.sbuf_tensor("Zcat", [128, DC * NTOK], BF16))
    xo_sb = Cd(nc.sbuf_tensor("xo_sb", [128, 4 * NTOK], F32))
    res1_sb = Cd(nc.sbuf_tensor("res1_sb", [128, DC * NTOK], BF16))
    wo_sb = Cd(nc.sbuf_tensor("wo_sb", [128, 2 * 2048], BF16))
    sq_sb = Cd(nc.sbuf_tensor("sq_sb", [128, 2 * NTOK], BF16))
    r_mean = Cd(nc.sbuf_tensor("r_mean", [1, NTOK], F32))
    r_var = Cd(nc.sbuf_tensor("r_var", [1, NTOK], F32))
    pack = Cd(nc.sbuf_tensor("pack", [1, 2 * NTOK], F32))
    pack_bf = Cd(nc.sbuf_tensor("pack_bf", [1, 2 * NTOK], BF16))
    tmp_f = Cd(nc.sbuf_tensor("tmp_f", [128, 2 * NTOK], F32))
    es_ps_s = ExitStack()
    sums_ps = es_ps_s.enter_context(nc.psum_tensor("sums_ps", [1, NTOK], F32))
    sqs_ps = es_ps_s.enter_context(nc.psum_tensor("sqs_ps", [1, NTOK], F32))
    es_ps_at = ExitStack()
    at_ps = [es_ps_at.enter_context(nc.psum_tensor(f"at_ps{i}", [128, NTOK], F32)) for i in range(2)]

    with nc.Block() as blk:
        @blk.sync
        def _(sy):
            for h in range(8):
                W(sy, "VE", f"ve_zev_{h * 4 + 3}")
                for half in range(2):
                    if h <= 5:
                        dst = zc1_in[h * 256 + half * 128:h * 256 + (half + 1) * 128, :]
                    else:
                        dst = zc2_in[(h - 6) * 256 + half * 128:(h - 6) * 256 + (half + 1) * 128, :]
                    DMA(sy, "LX", dst,
                        Z_sb[:, h * T + half * NTOK:h * T + half * NTOK + NTOK])
                if h == 5:
                    mark("lx_zstage1", "LX")
            mark("lx_zstage", "LX")
            if DEBUG:
                DMA(sy, "LX", dbg_z, Z_sb[:, :])
            W(sy, "MM", "mm_C")
            W(sy, "SE", "se_C")
            for ct in range(DC):
                if ct >= 4:
                    W(sy, "VE", f"ve_res1_{ct - 4}")
                DMA(sy, "LX", xo_sb[:, (ct % 4) * NTOK:(ct % 4 + 1) * NTOK],
                    xTo[ct * 128:(ct + 1) * 128, :], f"lx_xo_{ct}")
            if DEBUG:
                W(sy, "LD", "ld_zcat")
                DMA(sy, "LX", dbg_zcat, Zcat[:, :])
                W(sy, "VE", "ve_res1_all")
                DMA(sy, "LX", dbg_res1, res1_sb[:, :], "lx_dbgD")

        @blk.gpsimd
        def _(g):
            W(g, "LX", "lx_zstage1")
            g.collective_compute("AllGather", ALU.bypass,
                                 replica_groups=[[0, 1], [2, 3], [4, 5], [6, 7]],
                                 ins=[zc1_in.ap().opt()], outs=[zc1_out.ap().opt()]).then_inc(SEM["CC"], 1)
            inc("CC", "cc_z1")
            W(g, "LX", "lx_zstage")
            g.collective_compute("AllGather", ALU.bypass,
                                 replica_groups=[[0, 1], [2, 3], [4, 5], [6, 7]],
                                 ins=[zc2_in.ap().opt()], outs=[zc2_out.ap().opt()]).then_inc(SEM["CC"], 1)
            inc("CC", "cc_z2")
            W(g, "MM", "mm_C")
            W(g, "SE", "se_C")
            for ot in range(2):
                DMA(g, "LD", wo_sb[:, (ot % 2) * 2048:(ot % 2 + 1) * 2048], wo[ot], f"ld_wo_{ot}")
            for fl in range(2):
                DMA(g, "LD", wgu_sb[:, (fl % 2) * 2048:(fl % 2 + 1) * 2048],
                    wg[0, fl], f"ld_gu_{fl}")
            W(g, "CC", "cc_z1")
            for ctk in range(DC):
                if ctk % 8 == 6:
                    break
                g.indirect_dma_start(
                    out=Zcat[:, ctk * NTOK:(ctk + 1) * NTOK],
                    out_offset=None,
                    in_=zc1_out.ap(),
                    in_offset=IndirectOffsetOnAxis(ap=zidx_sb[:, ctk:ctk + 1], axis=0),
                ).then_inc(SEM["LD"], 16)
                inc("LD", None, 16)
            for ctk in range(DC):
                if ctk % 8 <= 5 and ctk != 7 and ctk != 6 and ctk < 8:
                    continue
                pass
            for ctk in [8, 9, 10, 11, 12, 13]:
                g.indirect_dma_start(
                    out=Zcat[:, ctk * NTOK:(ctk + 1) * NTOK],
                    out_offset=None,
                    in_=zc1_out.ap(),
                    in_offset=IndirectOffsetOnAxis(ap=zidx_sb[:, ctk:ctk + 1], axis=0),
                ).then_inc(SEM["LD"], 16)
                inc("LD", None, 16)
            mark("ld_zcat_a", "LD")
            W(g, "CC", "cc_z2")
            for ctk in [6, 7, 14, 15]:
                g.indirect_dma_start(
                    out=Zcat[:, ctk * NTOK:(ctk + 1) * NTOK],
                    out_offset=None,
                    in_=zc2_out.ap(),
                    in_offset=IndirectOffsetOnAxis(ap=zidx_sb[:, ctk:ctk + 1], axis=0),
                ).then_inc(SEM["LD"], 16)
                inc("LD", None, 16)
            mark("ld_zcat", "LD")
            for ot in range(2, DC):
                W(g, "MM", f"mm_at_{ot - 2}")
                DMA(g, "LD", wo_sb[:, (ot % 2) * 2048:(ot % 2 + 1) * 2048], wo[ot], f"ld_wo_{ot}")

        @blk.tensor
        def _(t):
            ZORD = [0, 1, 2, 3, 4, 5, 8, 9, 10, 11, 12, 13, 6, 7, 14, 15]

            def lnsums(po):
                W(t, "VE", f"ve_res1_{po}")
                for n in range(2):
                    ins = t.matmul(sums_ps[0:1, n * 512:(n + 1) * 512], onesc_sb[:, 0:1],
                                   res1_sb[:, po * NTOK + n * 512:po * NTOK + (n + 1) * 512],
                                   start=(po == 0), stop=(po == DC - 1))
                TI(ins, "MM", f"mm_sum_{po}")
                W(t, "SE", f"se_sq_{po}")
                for n in range(2):
                    ins = t.matmul(sqs_ps[0:1, n * 512:(n + 1) * 512], onesc_sb[:, 0:1],
                                   sq_sb[:, (po % 2) * NTOK + n * 512:(po % 2) * NTOK + (n + 1) * 512],
                                   start=(po == 0), stop=(po == DC - 1))
                TI(ins, "MM", f"mm_sq_{po}")

            W(t, "LD", "ld_zcat_a")
            for ot in range(DC):
                W(t, "LD", f"ld_wo_{ot}")
                if ot >= 2:
                    W(t, "VE", f"ve_res1_{ot - 2}")
                for ci, c in enumerate(ZORD):
                    if ot == 0 and ci == 12:
                        W(t, "LD", "ld_zcat")
                    for n in range(2):
                        ins = t.matmul(at_ps[ot % 2][:, n * 512:(n + 1) * 512],
                                       wo_sb[:, (ot % 2) * 2048 + c * 128:(ot % 2) * 2048 + (c + 1) * 128],
                                       Zcat[:, c * NTOK + n * 512:c * NTOK + (n + 1) * 512],
                                       start=(ci == 0), stop=(ci == DC - 1))
                TI(ins, "MM", f"mm_at_{ot}")
                if ot >= 2:
                    lnsums(ot - 2)
            for po in (DC - 2, DC - 1):
                lnsums(po)
            mark("mm_lnsums", "MM")

        @blk.vector
        def _(v):
            for ot in range(DC):
                W(v, "MM", f"mm_at_{ot}")
                W(v, "LX", f"lx_xo_{ot}")
                ins = v.tensor_tensor(res1_sb[:, ot * NTOK:(ot + 1) * NTOK],
                                      at_ps[ot % 2][:, :],
                                      xo_sb[:, (ot % 4) * NTOK:(ot % 4 + 1) * NTOK], ALU.add)
                TI(ins, "VE", f"ve_res1_{ot}")
            mark("ve_res1_all", "VE")

        @blk.scalar
        def _(s):
            for ot in range(DC):
                W(s, "VE", f"ve_res1_{ot}")
                if ot >= 2:
                    W(s, "MM", f"mm_sq_{ot - 2}")
                ins = s.square(sq_sb[:, (ot % 2) * NTOK:(ot % 2 + 1) * NTOK],
                               res1_sb[:, ot * NTOK:(ot + 1) * NTOK])
                TI(ins, "SE", f"se_sq_{ot}")

    es_ps_at.close()
    es_ps_ab = ExitStack()
    ab_ps = es_ps_ab.enter_context(nc.psum_tensor("ab_ps", [128, 2048], F32))

    with nc.Block() as blk:
        @blk.vector
        def _(v):
            W(v, "MM", "mm_lnsums")
            FENCE(v, v.tensor_scalar_mul(r_mean[0:1, :], sums_ps[0:1, :], 1.0 / D))
            FENCE(v, v.tensor_scalar_mul(r_var[0:1, :], sqs_ps[0:1, :], 1.0 / D))
            FENCE(v, v.tensor_tensor(pack[0:1, 0:NTOK], r_mean[0:1, :], r_mean[0:1, :], ALU.mult))
            ins = v.tensor_tensor(r_var[0:1, :], r_var[0:1, :], pack[0:1, 0:NTOK], ALU.subtract)
            TI(ins, "VE", "ve_var1")
            W(v, "SE", "se_std1")
            FENCE(v, v.reciprocal(pack[0:1, 0:NTOK], pack[0:1, NTOK:2 * NTOK]))
            FENCE(v, v.scalar_tensor_tensor(pack[0:1, NTOK:2 * NTOK], r_mean[0:1, :], -1.0,
                                            pack[0:1, 0:NTOK], ALU.mult, ALU.mult))
            ins = v.tensor_copy(pack_bf[0:1, :], pack[0:1, :])
            TI(ins, "VE", "ve_lnrows1")
            W(v, "MM", "mm_ab1")
            for ct in range(DC):
                v.tensor_tensor(tmp_f[:, (ct % 2) * NTOK:(ct % 2 + 1) * NTOK],
                                res1_sb[:, ct * NTOK:(ct + 1) * NTOK],
                                ab_ps[:, 0:NTOK], ALU.mult)
                ins = v.tensor_tensor(h1_sb[:, ct * NTOK:(ct + 1) * NTOK],
                                      tmp_f[:, (ct % 2) * NTOK:(ct % 2 + 1) * NTOK],
                                      ab_ps[:, NTOK:2 * NTOK], ALU.add)
                TI(ins, "VE", f"ve_h1_{ct}")
            mark("ve_h1", "VE")

        @blk.scalar
        def _(s):
            W(s, "VE", "ve_var1")
            ins = s.activation(pack[0:1, NTOK:2 * NTOK], r_var[0:1, :], AF.Sqrt,
                               bias=constf_sb[0:1, 0:1], scale=1.0)
            TI(ins, "SE", "se_std1")

        @blk.tensor
        def _(t):
            W(t, "VE", "ve_lnrows1")
            for n in range(4):
                ins = t.matmul(ab_ps[:, n * 512:(n + 1) * 512], onesr_sb[0:1, :],
                               pack_bf[0:1, n * 512:(n + 1) * 512], start=True, stop=True)
            TI(ins, "MM", "mm_ab1")

    es_d.close()
    es_z.close()
    es_ps_ab.close()
    es_ps_s.close()
    combbc_sb = C(nc.sbuf_tensor("combbc_sb", [128, E * NTOK], BF16))
    es_e = ExitStack()
    Ce = es_e.enter_context
    gates_sb = Ce(nc.sbuf_tensor("gates_sb", [128, 64], F32))
    oh1_sb = Ce(nc.sbuf_tensor("oh1_sb", [128, 64], BF16))
    oh2_sb = Ce(nc.sbuf_tensor("oh2_sb", [128, 64], BF16))
    v1_sb = Ce(nc.sbuf_tensor("v1_sb", [128, 64], BF16))
    v2_sb = Ce(nc.sbuf_tensor("v2_sb", [128, 64], BF16))
    sc8_sb = Ce(nc.sbuf_tensor("sc8_sb", [128, 8], F32))
    gm_sb = Ce(nc.sbuf_tensor("gm_sb", [128, 8], F32))
    top2_sb = Ce(nc.sbuf_tensor("top2_sb", [128, 32], F32))
    gm64_sb = Ce(nc.sbuf_tensor("gm64_sb", [128, 64], F32))
    rtA = Ce(nc.sbuf_tensor("rtA", [8, 4 * NTOK], F32))
    rtB = Ce(nc.sbuf_tensor("rtB", [8, 4 * NTOK], F32))
    combT_bf = Ce(nc.sbuf_tensor("combT_bf", [8, NTOK], BF16))
    cnt_sb = Ce(nc.sbuf_tensor("cnt_sb", [8, 2], F32))
    cnta_sb = Ce(nc.sbuf_tensor("cnta_sb", [64, 2], F32))
    off_sb = Ce(nc.sbuf_tensor("off_sb", [8, 4], F32))
    es_psE = ExitStack()
    lg_ps = es_psE.enter_context(nc.psum_tensor("lg_ps", [128, 64], F32))
    trE_ps = [es_psE.enter_context(nc.psum_tensor(f"trE_ps{i}", [8, 512], F32)) for i in range(2)]
    pmA_ps = es_psE.enter_context(nc.psum_tensor("pmA_ps", [8, 2], F32))
    pmB_ps = es_psE.enter_context(nc.psum_tensor("pmB_ps", [8, 2], F32))
    cb_ps = es_psE.enter_context(nc.psum_tensor("cb_ps", [128, NTOK], F32))

    with nc.Block() as blk:
        @blk.tensor
        def _(t):
            W(t, "VE", "ve_h1")
            for tt in range(8):
                for c in range(DC):
                    ins = t.matmul(lg_ps[:, tt * 8:(tt + 1) * 8],
                                   h1_sb[:, c * NTOK + tt * 128:c * NTOK + (tt + 1) * 128],
                                   wr_sb[:, c * 8:(c + 1) * 8],
                                   start=(c == 0), stop=(c == DC - 1))
            TI(ins, "MM", "mm_lg_all")
            W(t, "VE", "ve_top2all")
            for tt in range(8):
                if tt >= 2:
                    W(t, "VE", f"ve_trev_{tt - 2}")
                for ki, srcT in enumerate((oh1_sb, oh2_sb, v1_sb, v2_sb)):
                    ins = t.matmul(trE_ps[tt % 2][0:8, ki * 128:(ki + 1) * 128],
                                   srcT[:, tt * 8:(tt + 1) * 8], id_bf[:, :],
                                   start=True, stop=True)
                TI(ins, "MM", f"mm_tr_{tt}")
            W(t, "LD", "ld_cnta")
            t.matmul(pmA_ps[0:8, 0:2], pref_sb[:, :], cnta_sb[:, :], start=True, stop=True)
            ins = t.matmul(pmB_ps[0:8, 0:2], all_sb[:, :], cnta_sb[:, :], start=True, stop=True)
            TI(ins, "MM", "mm_pref")
            for e in range(E):
                W(t, "VE", "ve_combT")
                if e >= 1:
                    W(t, "SE", f"se_cb_{e - 1}")
                for n in range(2):
                    ins = t.matmul(cb_ps[:, n * 512:(n + 1) * 512],
                                   sel_sb[:, e * 128:(e + 1) * 128],
                                   combT_bf[0:8, n * 512:(n + 1) * 512],
                                   start=True, stop=True)
                TI(ins, "MM", f"mm_cb_{e}")
            mark("mm_E", "MM")

        @blk.vector
        def _(v):
            W(v, "MM", "mm_lg_all")
            ins = v.reduce_max(sc8_sb[:, 0:1], lg_ps[:, :], AX, negate=True)
            TI(ins, "VE", "ve_lgm")
            W(v, "SE", "se_gexp")
            for tt in range(8):
                ins = v.reduce_sum(top2_sb[:, tt:tt + 1], gates_sb[:, tt * 8:(tt + 1) * 8], AX)
            FENCE(v, ins)
            FENCE(v, v.reciprocal(top2_sb[:, 8:16], top2_sb[:, 0:8]))
            for tt in range(8):
                ins = v.tensor_scalar_mul(gates_sb[:, tt * 8:(tt + 1) * 8],
                                          gates_sb[:, tt * 8:(tt + 1) * 8], top2_sb[:, 8 + tt:9 + tt])
            FENCE(v, ins)
            for tt in range(8):
                ins = v.reduce_max(top2_sb[:, 16 + tt:17 + tt], gates_sb[:, tt * 8:(tt + 1) * 8], AX)
            FENCE(v, ins)
            for tt in range(8):
                ins = v.tensor_single_scalar(oh1_sb[:, tt * 8:(tt + 1) * 8],
                                             gates_sb[:, tt * 8:(tt + 1) * 8],
                                             top2_sb[:, 16 + tt:17 + tt], ALU.is_equal)
            FENCE(v, ins)
            FENCE(v, v.tensor_tensor(gm64_sb[:, :], gates_sb[:, :], oh1_sb[:, :], ALU.mult))
            FENCE(v, v.tensor_tensor(gm64_sb[:, :], gates_sb[:, :], gm64_sb[:, :], ALU.subtract))
            for tt in range(8):
                ins = v.reduce_max(top2_sb[:, 24 + tt:25 + tt], gm64_sb[:, tt * 8:(tt + 1) * 8], AX)
            FENCE(v, ins)
            for tt in range(8):
                ins = v.tensor_single_scalar(oh2_sb[:, tt * 8:(tt + 1) * 8],
                                             gm64_sb[:, tt * 8:(tt + 1) * 8],
                                             top2_sb[:, 24 + tt:25 + tt], ALU.is_equal)
            FENCE(v, ins)
            for tt in range(8):
                v.tensor_single_scalar(v1_sb[:, tt * 8:(tt + 1) * 8], oh1_sb[:, tt * 8:(tt + 1) * 8],
                                       top2_sb[:, 16 + tt:17 + tt], ALU.mult)
                v.tensor_single_scalar(v2_sb[:, tt * 8:(tt + 1) * 8], oh2_sb[:, tt * 8:(tt + 1) * 8],
                                       top2_sb[:, 24 + tt:25 + tt], ALU.mult)
            ins = v.reduce_sum(sc8_sb[:, 5:6], h1_sb[:, 0:NTOK], AX)
            TI(ins, "VE", "ve_top2all")
            for tt in range(8):
                W(v, "MM", f"mm_tr_{tt}")
                for ki in range(4):
                    ins = v.tensor_copy(rtA[0:8, ki * NTOK + tt * 128:ki * NTOK + (tt + 1) * 128],
                                        trE_ps[tt % 2][0:8, ki * 128:(ki + 1) * 128])
                TI(ins, "VE", f"ve_trev_{tt}")
            FENCE(v, v.tensor_tensor_scan(rtB[0:8, 0:NTOK], rtA[0:8, 0:NTOK], rtA[0:8, 0:NTOK],
                                          0.0, ALU.add, ALU.bypass))
            FENCE(v, v.tensor_tensor_scan(rtB[0:8, NTOK:2 * NTOK], rtA[0:8, NTOK:2 * NTOK],
                                          rtA[0:8, NTOK:2 * NTOK], 0.0, ALU.add, ALU.bypass))
            FENCE(v, v.tensor_copy(cnt_sb[0:8, 0:1], rtB[0:8, NTOK - 1:NTOK]))
            FENCE(v, v.tensor_copy(cnt_sb[0:8, 1:2], rtB[0:8, 2 * NTOK - 1:2 * NTOK]))
            ins = v.reduce_sum(sc8_sb[0:8, 6:7], rtB[0:8, 0:NTOK], AX)
            TI(ins, "VE", "ve_cnt")
            W(v, "MM", "mm_pref")
            FENCE(v, v.tensor_copy(off_sb[0:8, 0:2], pmA_ps[0:8, :]))
            FENCE(v, v.tensor_copy(off_sb[0:8, 2:3], pmB_ps[0:8, 0:1]))
            ins = v.tensor_tensor(off_sb[0:8, 3:4], off_sb[0:8, 2:3], off_sb[0:8, 1:2], ALU.add)
            FENCE(v, ins)
            mark("ve_off", "VE")
            FENCE(v, v.tensor_scalar(rtB[0:8, 0:NTOK], rtB[0:8, 0:NTOK], off_sb[0:8, 0:1], CAP,
                                     ALU.add, ALU.is_le))
            FENCE(v, v.tensor_scalar(rtB[0:8, NTOK:2 * NTOK], rtB[0:8, NTOK:2 * NTOK], off_sb[0:8, 3:4], CAP,
                                     ALU.add, ALU.is_le))
            FENCE(v, v.tensor_tensor(rtB[0:8, 2 * NTOK:3 * NTOK], rtA[0:8, 2 * NTOK:3 * NTOK],
                                     rtB[0:8, 0:NTOK], ALU.mult))
            FENCE(v, v.tensor_tensor(rtB[0:8, 3 * NTOK:4 * NTOK], rtA[0:8, 3 * NTOK:4 * NTOK],
                                     rtB[0:8, NTOK:2 * NTOK], ALU.mult))
            FENCE(v, v.tensor_tensor(rtB[0:8, 2 * NTOK:3 * NTOK], rtB[0:8, 2 * NTOK:3 * NTOK],
                                     rtB[0:8, 3 * NTOK:4 * NTOK], ALU.add))
            ins = v.tensor_copy(combT_bf[0:8, :], rtB[0:8, 2 * NTOK:3 * NTOK])
            TI(ins, "VE", "ve_combT")
            mark("ve_E", "VE")

        @blk.scalar
        def _(s):
            W(s, "VE", "ve_lgm")
            s.activation(gates_sb[:, :], lg_ps[:, :],
                         AF.Exp, bias=sc8_sb[:, 0:1], scale=1.0)
            ins = s.copy(gm_sb[:, :], gates_sb[:, 0:8])
            TI(ins, "SE", "se_gexp")
            for e in range(E):
                W(s, "MM", f"mm_cb_{e}")
                ins = s.copy(combbc_sb[:, e * NTOK:(e + 1) * NTOK], cb_ps[:, :])
                TI(ins, "SE", f"se_cb_{e}")
            mark("se_E", "SE")

        if DEBUG:
            @blk.sync
            def _(sy):
                W(sy, "VE", "ve_combT")
                W(sy, "LD", "ld_cnta")
                DMA(sy, "LX", dbg_gates, gates_sb[:, :])
                DMA(sy, "LX", dbg_rtA, rtA[0:8, :])
                DMA(sy, "LX", dbg_rtB, rtB[0:8, :])
                DMA(sy, "LX", dbg_off, off_sb[0:8, :])
                DMA(sy, "LX", dbg_cnta, cnta_sb[:, :], "lx_dbgE")

        @blk.gpsimd
        def _(g):
            W(g, "VE", "ve_cnt")
            DMA(g, "LD", cnt_in.ap(), cnt_sb[0:8, :], "ld_cntin")
            W(g, "LD", "ld_cntin")
            g.collective_compute("AllGather", ALU.bypass,
                                 replica_groups=[[0, 1, 2, 3, 4, 5, 6, 7]],
                                 ins=[cnt_in.ap().opt()], outs=[cnt_out.ap().opt()]).then_inc(SEM["CC"], 1)
            inc("CC", "cc_cnt")
            W(g, "CC", "cc_cnt")
            DMA(g, "LD", cnta_sb[:, :], cnt_out.ap(), "ld_cnta")

    es_e.close()
    es_psE.close()
    es_f = ExitStack()
    Cf = es_f.enter_context
    y_sbt = C(nc.sbuf_tensor("y_sbt", [128, DC * NTOK], F32))
    G_sb = Cf(nc.sbuf_tensor("G_sb", [128, IC * NTOK], BF16))
    GU_sb = Cf(nc.sbuf_tensor("GU_sb", [128, IC * NTOK], BF16))
    es_psF = ExitStack()
    gu_ps = [es_psF.enter_context(nc.psum_tensor(f"gu_ps{i}", [128, NTOK], F32)) for i in range(2)]
    d_ps = [es_psF.enter_context(nc.psum_tensor(f"d_ps{i}", [128, NTOK], F32)) for i in range(2)]

    def flkind(fl):
        return "G" if (fl // IC) % 2 == 0 else "U"

    with nc.Block() as blk:
        @blk.gpsimd
        def _(g):
            if DEBUG:
                W(g, "LX", "lx_dbgD")
                W(g, "LX", "lx_dbgE")
            for e in range(E):
                for pi, wsrc in ((0, wg), (1, wu)):
                    for it in range(IC):
                        fl = (e * 2 + pi) * IC + it
                        if fl < 2:
                            continue
                        W(g, "MM", f"mm_gu_{fl - 2}")
                        DMA(g, "LD", wgu_sb[:, (fl % 2) * 2048:(fl % 2 + 1) * 2048],
                            wsrc[e, it], f"ld_gu_{fl}")
                for ot in range(DC):
                    k = e * DC + ot
                    if k >= 2:
                        W(g, "MM", f"mm_d_{k - 2}")
                    DMA(g, "LD", wd_sb[:, (k % 2) * IC * 128:(k % 2 + 1) * IC * 128],
                        wd[e, ot], f"ld_d_{k}")

        @blk.tensor
        def _(t):
            W(t, "VE", "ve_h1")
            W(t, "SE", "se_E")
            W(t, "VE", "ve_combT")
            for e in range(E):
                for pi in range(2):
                    for it in range(IC):
                        fl = (e * 2 + pi) * IC + it
                        W(t, "LD", f"ld_gu_{fl}")
                        if fl >= 2:
                            pfl = fl - 2
                            if flkind(pfl) == "G":
                                W(t, "SE", f"se_gfl_{pfl}")
                            else:
                                W(t, "VE", f"ve_gufl_{pfl}")
                        if pi == 0 and it == 0 and e >= 1:
                            W(t, "MM", f"mm_d_{e * DC - 1}")
                        for c in range(DC):
                            for n in range(2):
                                ins = t.matmul(gu_ps[fl % 2][:, n * 512:(n + 1) * 512],
                                               wgu_sb[:, (fl % 2) * 2048 + c * 128:(fl % 2) * 2048 + (c + 1) * 128],
                                               h1_sb[:, c * NTOK + n * 512:c * NTOK + (n + 1) * 512],
                                               start=(c == 0), stop=(c == DC - 1))
                        TI(ins, "MM", f"mm_gu_{fl}")
                for ot in range(DC):
                    k = e * DC + ot
                    W(t, "LD", f"ld_d_{k}")
                    W(t, "VE", f"ve_gufl_{(e * 2 + 1) * IC + IC - 1}")
                    if k >= 2:
                        W(t, "VE", f"ve_y_{k - 2}")
                    for c in range(IC):
                        for n in range(2):
                            ins = t.matmul(d_ps[k % 2][:, n * 512:(n + 1) * 512],
                                           wd_sb[:, (k % 2) * IC * 128 + c * 128:(k % 2) * IC * 128 + (c + 1) * 128],
                                           G_sb[:, c * NTOK + n * 512:c * NTOK + (n + 1) * 512],
                                           start=(c == 0), stop=(c == IC - 1))
                    TI(ins, "MM", f"mm_d_{k}")
            mark("mm_F", "MM")

        @blk.scalar
        def _(s):
            for e in range(E):
                for it in range(IC):
                    fl = (e * 2) * IC + it
                    W(s, "MM", f"mm_gu_{fl}")
                    if e >= 1 and it == 0:
                        W(s, "MM", f"mm_d_{e * DC - 1}")
                    ins = s.activation(G_sb[:, it * NTOK:(it + 1) * NTOK],
                                       gu_ps[fl % 2][:, :], AF.Silu)
                    TI(ins, "SE", f"se_gfl_{fl}")
            mark("se_F", "SE")

        @blk.vector
        def _(v):
            for e in range(E):
                for it in range(IC):
                    flg = (e * 2) * IC + it
                    flu = (e * 2 + 1) * IC + it
                    W(v, "MM", f"mm_gu_{flu}")
                    W(v, "SE", f"se_gfl_{flg}")
                    if it == 0:
                        W(v, "SE", f"se_cb_{e}")
                    v.tensor_tensor(GU_sb[:, it * NTOK:(it + 1) * NTOK],
                                    gu_ps[flu % 2][:, :], G_sb[:, it * NTOK:(it + 1) * NTOK], ALU.mult)
                    ins = v.tensor_tensor(G_sb[:, it * NTOK:(it + 1) * NTOK],
                                          GU_sb[:, it * NTOK:(it + 1) * NTOK],
                                          combbc_sb[:, e * NTOK:(e + 1) * NTOK], ALU.mult)
                    TI(ins, "VE", f"ve_gufl_{flu}")
                    if flu == (e * 2 + 1) * IC + IC - 1:
                        pass
                for ot in range(DC):
                    k = e * DC + ot
                    W(v, "MM", f"mm_d_{k}")
                    if e == 0:
                        ins = v.tensor_copy(y_sbt[:, ot * NTOK:(ot + 1) * NTOK], d_ps[k % 2][:, :])
                    else:
                        ins = v.tensor_tensor(y_sbt[:, ot * NTOK:(ot + 1) * NTOK],
                                              d_ps[k % 2][:, :],
                                              y_sbt[:, ot * NTOK:(ot + 1) * NTOK], ALU.add)
                    TI(ins, "VE", f"ve_y_{k}")
            mark("ve_F", "VE")

    es_f.close()
    es_psF.close()
    es_g = ExitStack()
    Cg = es_g.enter_context
    r2b_sb = Cg(nc.sbuf_tensor("r2b_sb", [128, 2 * NTOK], BF16))
    sq2_sb = Cg(nc.sbuf_tensor("sq2_sb", [128, 2 * NTOK], BF16))
    rows_sb = Cg(nc.sbuf_tensor("rows_sb", [128, NTOK], F32))
    nmcol_sb = Cg(nc.sbuf_tensor("nmcol_sb", [128, 8], F32))
    rscol_sb = Cg(nc.sbuf_tensor("rscol_sb", [128, 8], F32))
    out_sb = Cg(nc.sbuf_tensor("out_sb", [128, 2 * D], F32))
    r_mean2 = Cg(nc.sbuf_tensor("r_mean2", [1, NTOK], F32))
    r_var2 = Cg(nc.sbuf_tensor("r_var2", [1, NTOK], F32))
    r_tmp2 = Cg(nc.sbuf_tensor("r_tmp2", [1, NTOK], F32))
    es_psG1 = ExitStack()
    sums2_ps = es_psG1.enter_context(nc.psum_tensor("sums2_ps", [1, NTOK], F32))
    sqs2_ps = es_psG1.enter_context(nc.psum_tensor("sqs2_ps", [1, NTOK], F32))
    rt_ps = es_psG1.enter_context(nc.psum_tensor("rt_ps", [128, 128], F32))

    with nc.Block() as blk:
        if DEBUG:
            @blk.sync
            def _(sy):
                W(sy, "VE", "ve_F")
                DMA(sy, "LX", dbg_y, y_sbt[:, :], "lx_dbgy")
                W(sy, "VE", "ve_h1")
                DMA(sy, "LX", dbg_h1, h1_sb[:, :], None)
                W(sy, "VE", "ve_combT")
                DMA(sy, "LX", dbg_comb, combbc_sb[0:1, :], "lx_dbgG")

        @blk.vector
        def _(v):
            W(v, "MM", "mm_F")
            if DEBUG:
                W(v, "LX", "lx_dbgy")
            for ct in range(DC):
                v.tensor_tensor(y_sbt[:, ct * NTOK:(ct + 1) * NTOK],
                                y_sbt[:, ct * NTOK:(ct + 1) * NTOK],
                                h1_sb[:, ct * NTOK:(ct + 1) * NTOK], ALU.add)
                if ct >= 2:
                    W(v, "MM", f"mm_s2_{ct - 2}")
                    W(v, "SE", f"se_sq2_{ct - 2}")
                ins = v.tensor_copy(r2b_sb[:, (ct % 2) * NTOK:(ct % 2 + 1) * NTOK],
                                    y_sbt[:, ct * NTOK:(ct + 1) * NTOK])
                TI(ins, "VE", f"ve_r2_{ct}")
            W(v, "MM", "mm_lnsums2")
            FENCE(v, v.tensor_scalar_mul(r_mean2[0:1, :], sums2_ps[0:1, :], 1.0 / D))
            FENCE(v, v.tensor_scalar_mul(r_var2[0:1, :], sqs2_ps[0:1, :], 1.0 / D))
            FENCE(v, v.tensor_tensor(r_tmp2[0:1, :], r_mean2[0:1, :], r_mean2[0:1, :], ALU.mult))
            v.memset(rows_sb[:, :], 0.0)
            ins = v.tensor_tensor(r_var2[0:1, :], r_var2[0:1, :], r_tmp2[0:1, :], ALU.subtract)
            TI(ins, "VE", "ve_var2")
            W(v, "SE", "se_std2")
            FENCE(v, v.reciprocal(rows_sb[32:33, :], r_tmp2[0:1, :]))
            ins = v.tensor_scalar_mul(rows_sb[0:1, :], r_mean2[0:1, :], -1.0)
            TI(ins, "VE", "ve_rows2")
            for tt in range(8):
                W(v, "MM", f"mm_rt_{tt}")
                FENCE(v, v.tensor_copy(nmcol_sb[:, tt:tt + 1], rt_ps[:, 0:1]))
                ins = v.tensor_copy(rscol_sb[:, tt:tt + 1], rt_ps[:, 32:33])
                TI(ins, "VE", f"ve_cols_{tt}")

        @blk.scalar
        def _(s):
            for ct in range(DC):
                W(s, "VE", f"ve_r2_{ct}")
                if ct >= 2:
                    W(s, "MM", f"mm_sq2_{ct - 2}")
                ins = s.square(sq2_sb[:, (ct % 2) * NTOK:(ct % 2 + 1) * NTOK],
                               r2b_sb[:, (ct % 2) * NTOK:(ct % 2 + 1) * NTOK])
                TI(ins, "SE", f"se_sq2_{ct}")
            W(s, "VE", "ve_var2")
            ins = s.activation(r_tmp2[0:1, :], r_var2[0:1, :], AF.Sqrt,
                               bias=constf_sb[0:1, 0:1], scale=1.0)
            TI(ins, "SE", "se_std2")

        @blk.tensor
        def _(t):
            for ct in range(DC):
                W(t, "VE", f"ve_r2_{ct}")
                for n in range(2):
                    ins = t.matmul(sums2_ps[0:1, n * 512:(n + 1) * 512], onesc_sb[:, 0:1],
                                   r2b_sb[:, (ct % 2) * NTOK + n * 512:(ct % 2) * NTOK + (n + 1) * 512],
                                   start=(ct == 0), stop=(ct == DC - 1))
                TI(ins, "MM", f"mm_s2_{ct}")
                W(t, "SE", f"se_sq2_{ct}")
                for n in range(2):
                    ins = t.matmul(sqs2_ps[0:1, n * 512:(n + 1) * 512], onesc_sb[:, 0:1],
                                   sq2_sb[:, (ct % 2) * NTOK + n * 512:(ct % 2) * NTOK + (n + 1) * 512],
                                   start=(ct == 0), stop=(ct == DC - 1))
                TI(ins, "MM", f"mm_sq2_{ct}")
            mark("mm_lnsums2", "MM")
            W(t, "VE", "ve_rows2")
            for tt in range(8):
                if tt >= 1:
                    W(t, "VE", f"ve_cols_{tt - 1}")
                ins = t.transpose(rt_ps[:, :], rows_sb[:, tt * 128:(tt + 1) * 128], id_f[:, :])
                TI(ins, "MM", f"mm_rt_{tt}")

    es_psG1.close()
    es_psG2 = ExitStack()
    oT_ps = [es_psG2.enter_context(nc.psum_tensor(f"oT_ps{i}", [128, D], F32)) for i in range(2)]

    with nc.Block() as blk:
        @blk.tensor
        def _(t):
            W(t, "VE", "ve_cols_7")
            for tt in range(8):
                if tt >= 2:
                    W(t, "VE", f"ve_out_{tt - 2}")
                for ct in range(DC):
                    ins = t.transpose(oT_ps[tt % 2][:, ct * 128:(ct + 1) * 128],
                                      y_sbt[:, ct * NTOK + tt * 128:ct * NTOK + (tt + 1) * 128],
                                      id_f[:, :])
                TI(ins, "MM", f"mm_oT_{tt}")

        @blk.vector
        def _(v):
            for tt in range(8):
                W(v, "MM", f"mm_oT_{tt}")
                W(v, "VE", f"ve_cols_{tt}")
                if tt >= 2:
                    W(v, "OUT", f"out_{tt - 2}")
                ins = v.tensor_scalar(out_sb[:, (tt % 2) * D:(tt % 2 + 1) * D],
                                      oT_ps[tt % 2][:, :],
                                      nmcol_sb[:, tt:tt + 1], rscol_sb[:, tt:tt + 1],
                                      ALU.add, ALU.mult)
                TI(ins, "VE", f"ve_out_{tt}")

        @blk.sync
        def _(sy):
            for tt in range(8):
                W(sy, "VE", f"ve_out_{tt}")
                DMA(sy, "OUT", y_out[tt * 128:(tt + 1) * 128, :],
                    out_sb[:, (tt % 2) * D:(tt % 2 + 1) * D], f"out_{tt}")
            W(sy, "OUT", "out_7")

    es_psG2.close()
    es_g.close()
    es.close()
    return nc, rec


def _prep_in_maps(L0_attn_x, Wq, Wk, Wv, Wo, Wr, Wg, Wu, Wd):
    x = np.ascontiguousarray(np.asarray(L0_attn_x, dtype=np.float32))
    Wq = np.asarray(Wq, dtype=np.float32)
    Wk = np.asarray(Wk, dtype=np.float32)
    Wv = np.asarray(Wv, dtype=np.float32)
    Wo = np.asarray(Wo, dtype=np.float32)
    Wr = np.asarray(Wr, dtype=np.float32)
    Wg = np.asarray(Wg, dtype=np.float32)
    Wu = np.asarray(Wu, dtype=np.float32)
    Wd = np.asarray(Wd, dtype=np.float32)

    def col_tiles(Wmat, nout_t, nin_t):
        r = Wmat.reshape(nout_t, 128, nin_t, 128).transpose(0, 3, 2, 1)
        return np.ascontiguousarray(r.reshape(nout_t, 128, nin_t * 128))

    wq_t = col_tiles(Wq, DC, DC)
    wk_t = col_tiles(Wk, DC, DC)
    wv_t = col_tiles(Wv, DC, DC).reshape(DC, 128, DC, 128)
    wo_t = col_tiles(Wo, DC, DC)
    wv4_all = []
    for hg in range(2):
        sl = wv_t[hg * 8:(hg + 1) * 8]
        r = sl.reshape(2, 4, 128, DC, 128).transpose(0, 2, 3, 1, 4)
        wv4_all.append(np.ascontiguousarray(r.reshape(2, 128, DC * 512)))
    wr_t = np.ascontiguousarray(Wr.reshape(8, DC, 128).transpose(2, 1, 0).reshape(128, DC * 8))
    wg_t = np.ascontiguousarray(
        Wg.reshape(E, IC, 128, DC, 128).transpose(0, 1, 4, 3, 2).reshape(E, IC, 128, DC * 128))
    wu_t = np.ascontiguousarray(
        Wu.reshape(E, IC, 128, DC, 128).transpose(0, 1, 4, 3, 2).reshape(E, IC, 128, DC * 128))
    wd_t = np.ascontiguousarray(
        Wd.reshape(E, DC, 128, IC, 128).transpose(0, 1, 4, 3, 2).reshape(E, DC, 128, IC * 128))

    idbf = np.eye(128, dtype=ml_dtypes.bfloat16)
    idf32 = np.eye(128, dtype=np.float32)
    maskd = np.triu(np.full((128, 128), NEG, dtype=np.float32), k=1)
    onesr = np.ones((1, 128), dtype=ml_dtypes.bfloat16)
    onescb = np.ones((128, 1), dtype=ml_dtypes.bfloat16)
    seld = np.zeros((8, E * 128), dtype=ml_dtypes.bfloat16)
    for e in range(E):
        seld[e, e * 128:(e + 1) * 128] = 1
    alld = np.zeros((64, 8), dtype=np.float32)
    for r_ in range(8):
        for e_ in range(8):
            alld[8 * r_ + e_, e_] = 1.0

    in_maps = []
    xT_by_b = [np.ascontiguousarray(x[b].T) for b in range(B)]
    for c in range(8):
        b, hg = c // 2, c % 2
        prefd_c = np.zeros((64, 8), dtype=np.float32)
        for r_ in range(c):
            for e_ in range(8):
                prefd_c[8 * r_ + e_, e_] = 1.0
        zidx = np.zeros((128, DC), dtype=np.int32)
        for ctk in range(DC):
            rank, h = ctk // 8, ctk % 8
            if h <= 5:
                zidx[:, ctk] = rank * 1536 + h * 256 + hg * 128 + np.arange(128)
            else:
                zidx[:, ctk] = rank * 512 + (h - 6) * 256 + hg * 128 + np.arange(128)
        in_maps.append({
            "xT": xT_by_b[b],
            "xTo": np.ascontiguousarray(xT_by_b[b][:, hg * NTOK:(hg + 1) * NTOK]),
            "wq": np.ascontiguousarray(wq_t[hg * 8:(hg + 1) * 8]),
            "wk": np.ascontiguousarray(wk_t[hg * 8:(hg + 1) * 8]),
            "wv4": wv4_all[hg],
            "wo": wo_t,
            "wr": wr_t,
            "wg": wg_t,
            "wu": wu_t,
            "wd": wd_t,
            "idbf": idbf,
            "idf32": idf32,
            "maskd": maskd,
            "onesr": onesr,
            "onescb": onescb,
            "seld": seld,
            "prefd": prefd_c,
            "alld": alld,
            "zidxd": zidx,
            "constf": np.array([[EPS, 0.0, 0.0, 0.0]], dtype=np.float32),
        })
    return in_maps


_NC_CACHE = None


def _get_nc():
    global _NC_CACHE
    if _NC_CACHE is None:
        _, rec = build_program({}, False)
        nc, _ = build_program(rec, True)
        _NC_CACHE = nc
    return _NC_CACHE


def kernel(L0_attn_x, L0_attn_Wq, L0_attn_Wk, L0_attn_Wv, L0_attn_Wo,
           L0_ffn_Wr, L0_ffn_Wg, L0_ffn_Wu, L0_ffn_Wd):
    nc = _get_nc()
    in_maps = _prep_in_maps(L0_attn_x, L0_attn_Wq, L0_attn_Wk, L0_attn_Wv, L0_attn_Wo,
                            L0_ffn_Wr, L0_ffn_Wg, L0_ffn_Wu, L0_ffn_Wd)
    res = run_bass_kernel_spmd(nc, in_maps, core_ids=list(range(8)))
    out = np.empty((B, T, D), dtype=np.float32)
    for c in range(8):
        b, hg = c // 2, c % 2
        out[b, hg * NTOK:(hg + 1) * NTOK, :] = res.results[c]["y_out"]
    return out
